# version 01
# baseline (speedup 1.0000x reference)
import numpy as np
import contextlib
import concourse.bass as bass
import concourse.mybir as mybir

F32 = mybir.dt.float32
I32 = mybir.dt.int32
BF16 = mybir.dt.bfloat16

H, DH, SEG, D, N = 16, 64, 32, 1024, 4096
CHUNK, NCH = 256, 16


def host_constants():
    c = {}
    perm = np.concatenate([np.arange(0, 64, 2), np.arange(1, 64, 2)])
    c["perm_head"] = perm
    pos = np.arange(SEG, dtype=np.float64)
    freqs = 10000.0 ** (-np.arange(0, 64, 2, dtype=np.float64) / 64)
    ang = freqs[:, None] * pos[None, :]
    cos = np.cos(ang)
    sin = np.sin(ang)
    cos64 = np.concatenate([cos, cos], 0)
    sin64 = np.concatenate([-sin, sin], 0)
    cos128 = np.concatenate([cos64, cos64], 0)
    sin128 = np.concatenate([sin64, sin64], 0)
    c["cos_hi"] = np.ascontiguousarray(cos128.astype(np.float32))
    c["cos_lo"] = np.ascontiguousarray((cos128 - c["cos_hi"].astype(np.float64)).astype(np.float32))
    c["sin_hi"] = np.ascontiguousarray(sin128.astype(np.float32))
    c["sin_lo"] = np.ascontiguousarray((sin128 - c["sin_hi"].astype(np.float64)).astype(np.float32))
    kt = np.arange(128)[:, None]
    qt = np.arange(128)[None, :]
    allowed = (kt // SEG == qt // SEG) & (kt <= qt)
    c["maska"] = np.ascontiguousarray(np.where(allowed, 0.0, -1e30).astype(np.float32))
    import ml_dtypes
    c["ident"] = np.ascontiguousarray(np.eye(128, dtype=np.float32).astype(ml_dtypes.bfloat16))
    c["identf"] = np.ascontiguousarray(np.eye(128, dtype=np.float32))
    return c


def permute_wqkv(w_qkv):
    perm = host_constants()["perm_head"]
    cols = np.arange(3072)
    for h in range(H):
        cols[64 * h: 64 * h + 64] = 64 * h + perm
        cols[1024 + 64 * h: 1024 + 64 * h + 64] = 1024 + 64 * h + perm
    return np.ascontiguousarray(w_qkv[:, cols])


class P:
    ENGS = ("t", "v", "a", "g", "s")

    def __init__(self, nc, stack):
        self.nc = nc
        self.sem = {e: stack.enter_context(nc.semaphore("sem_" + e)) for e in self.ENGS}
        self.dsem = stack.enter_context(nc.semaphore("sem_dma"))
        self.dsem_g = stack.enter_context(nc.semaphore("sem_dmag"))
        self.dcnt_g = 0
        self.cnt = {e: 0 for e in self.ENGS}
        self.dcnt = 0
        self.eng = {"t": nc.tensor, "v": nc.vector, "a": nc.scalar,
                    "g": nc.gpsimd, "s": nc.sync}
        self.last = {e: None for e in self.ENGS}
        self.need_wait = {e: {} for e in self.ENGS}

    def emit(self, e, instr):
        if instr is not None:
            self.last[e] = instr
        return instr

    def _flush_waits(self, e):
        nw = self.need_wait[e]
        for o, cval in sorted(nw.items()):
            if o == "d":
                self.eng[e].wait_ge(self.dsem, 16 * cval)
            elif o == "dg":
                self.eng[e].wait_ge(self.dsem_g, 16 * cval)
            else:
                self.eng[e].wait_ge(self.sem[o], cval)
        self.need_wait[e] = {}

    def pre(self, e):
        self._flush_waits(e)
        return self.eng[e]

    def fence(self, *engines):
        for e in engines:
            assert self.last[e] is not None, f"fence: no instr pending on {e}"
            self.last[e].then_inc(self.sem[e], 1)
            self.cnt[e] += 1
            self.last[e] = None
        for w in self.ENGS:
            for e in engines:
                self.need_wait[w][e] = self.cnt[e]
        return self

    def dma(self, out, in_, queue="s", allow_slow=False):
        eng = self.pre(queue)
        instr = eng.dma_start(out=out, in_=in_, allow_slow_non_contiguous=allow_slow)
        if queue == "g":
            instr.then_inc(self.dsem_g, 16)
            self.dcnt_g += 1
        else:
            instr.then_inc(self.dsem, 16)
            self.dcnt += 1
        return instr

    def barrier_dma(self):
        for w in self.ENGS:
            self.need_wait[w]["d"] = self.dcnt
            if self.dcnt_g:
                self.need_wait[w]["dg"] = self.dcnt_g


BUFS = [
    ("WQKV", (128, 8 * 3072), BF16),
    ("WOUT", (128, 8 * 1024), BF16),
    ("QRHS", (128, 8 * 1024), BF16),
    ("RSB", (128, 1024), F32),
    ("INVSB", (128, 1024), F32),
    ("VBIASB", (128, 1024), F32),
    ("GCOL", (128, 8), F32),
    ("BCOL", (128, 8), F32),
    ("BIASF", (128, 16), F32),
    ("BVROWF", (1, 1024), F32),
    ("ONER", (1, 128), BF16),
    ("ONESF", (1, 128), F32),
    ("BETA", (128, 16), F32),
    ("BETB", (128, 16), F32),
    ("COS2", (128, 32), BF16),
    ("SIN2", (128, 32), BF16),
    ("MASKA", (128, 128), F32),
    ("IDENT", (128, 128), BF16),
    ("IDENTF", (128, 128), F32),
    ("STF", (128, 520), F32),
    ("STBP", (128, 1040), BF16),
    ("C64", (128, 64), F32),
    ("C64B", (128, 64), BF16),
    ("RUF", (128, 16), F32),
    ("RQF", (128, 16), F32),
    ("RUB", (128, 16), BF16),
    ("RQB", (128, 16), BF16),
    ("RSBC", (128, 16), BF16),
    ("XT", (128, 2048), F32),
    ("X1", (128, 2048), F32),
    ("XN2", (128, 2048), F32),
    ("FOUT", (128, 2048), F32),
    ("XN", (128, 2048), BF16),
    ("XNT", (128, 2048), BF16),
    ("QN", (128, 2048), BF16),
    ("KN", (128, 2048), BF16),
    ("QR", (128, 2048), BF16),
    ("KR", (128, 2048), BF16),
    ("SQ", (128, 2048), BF16),
    ("SK", (128, 2048), BF16),
    ("VB", (128, 2080), BF16),
    ("GHB", (128, 4096), BF16),
    ("UB", (128, 1040), BF16),
    ("TBF", (128, 1040), BF16),
    ("OT", (128, 2048), BF16),
    ("QI", (128, 1024), I32),
    ("TQ", (128, 1024), BF16),
    ("RT256", (128, 256), BF16),
    ("TMPA", (128, 1024), BF16),
    ("TMPB", (128, 1024), BF16),
    ("FTMP", (128, 1024), F32),
    ("ROWF", (8, 1024), F32),
]
ALIASES = {"QLTOK": "XN", "QLT": "OT", "ROWS": "FTMP", "JUNK": "TQ",
           "OB": "XNT", "SKT": "KN", "EB": "QN"}


def build_core(debug_dump=(), n_chunks=NCH):
    nc = bass.Bass("TRN2", target_bir_lowering=False)
    L = {}
    L["dx"] = nc.dram_tensor("x", [N, D], F32, kind="ExternalInput").ap()
    L["dwqkv"] = nc.dram_tensor("wqkv", [D, 3072], F32, kind="ExternalInput").ap()
    L["dwout"] = nc.dram_tensor("wout", [D, D], F32, kind="ExternalInput").ap()
    L["dg"] = nc.dram_tensor("ln_g", [D], F32, kind="ExternalInput").ap()
    L["db"] = nc.dram_tensor("ln_b", [D], F32, kind="ExternalInput").ap()
    L["dbeta"] = nc.dram_tensor("mem_beta", [H], F32, kind="ExternalInput").ap()
    L["drhs"] = nc.dram_tensor("rhs", [D, D], F32, kind="ExternalInput").ap()
    L["dch"] = nc.dram_tensor("cos_hi", [128, 32], F32, kind="ExternalInput").ap()
    L["dcl"] = nc.dram_tensor("cos_lo", [128, 32], F32, kind="ExternalInput").ap()
    L["dsh"] = nc.dram_tensor("sin_hi", [128, 32], F32, kind="ExternalInput").ap()
    L["dsl"] = nc.dram_tensor("sin_lo", [128, 32], F32, kind="ExternalInput").ap()
    L["dmask"] = nc.dram_tensor("maska", [128, 128], F32, kind="ExternalInput").ap()
    L["dident"] = nc.dram_tensor("ident", [128, 128], BF16, kind="ExternalInput").ap()
    L["didentf"] = nc.dram_tensor("identf", [128, 128], F32, kind="ExternalInput").ap()
    L["dout"] = nc.dram_tensor("out", [N, D], F32, kind="ExternalOutput").ap()
    L["ddbg"] = {}
    L["debug_dump"] = list(debug_dump)
    L["n_chunks"] = n_chunks
    L["dump_psum"] = any(n == "PSA" for n, _, _, _ in debug_dump)
    L["dump_comb"] = any(n == "TMA" for n, _, _, _ in debug_dump)
    for name, buf, shape, dt in debug_dump:
        L["ddbg"][name] = nc.dram_tensor("dbg_" + name, list(shape), dt,
                                         kind="ExternalOutput").ap()

    with contextlib.ExitStack() as st_:
        sb = {}
        for name, shape, dt in BUFS:
            sb[name] = st_.enter_context(nc.sbuf_tensor(name, list(shape), dt))
        for al, tgt in ALIASES.items():
            sb[al] = sb[tgt]
        for nm in ("pa0", "pa1", "pb0", "pb1", "pp0", "pp1", "pp2"):
            sb[nm] = st_.enter_context(nc.psum_tensor(nm, [128, 512], F32))
        sb["ptr"] = st_.enter_context(nc.psum_tensor("ptr", [128, 1024], BF16))
        p = P(nc, st_)
        with nc.Block() as block:
            @block.sync
            def _(sync):
                prologue(nc, p, sb, L)
                for c in range(n_chunks):
                    chunk_stages(nc, p, sb, L, c)
                epilogue(nc, p, sb, L)
    return nc


def make_inputs(x_b, w_qkv, w_out, ln_g, ln_b, mem_beta, rhs):
    c = host_constants()
    return {
        "x": np.ascontiguousarray(x_b.astype(np.float32)),
        "wqkv": permute_wqkv(np.asarray(w_qkv, dtype=np.float32)),
        "wout": np.ascontiguousarray(np.asarray(w_out, dtype=np.float32)),
        "ln_g": np.ascontiguousarray(np.asarray(ln_g, dtype=np.float32)),
        "ln_b": np.ascontiguousarray(np.asarray(ln_b, dtype=np.float32)),
        "mem_beta": np.ascontiguousarray(np.asarray(mem_beta, dtype=np.float32).reshape(H)),
        "rhs": np.ascontiguousarray(np.asarray(rhs, dtype=np.float32)),
        "cos_hi": c["cos_hi"], "cos_lo": c["cos_lo"],
        "sin_hi": c["sin_hi"], "sin_lo": c["sin_lo"],
        "maska": c["maska"], "ident": c["ident"], "identf": c["identf"],
    }


AF = mybir.ActivationFunctionType
ALU = mybir.AluOpType
NTT, NBL, NF = 2, 2, 8





RHSBUFS = [("XT", 0), ("XT", 1), ("X1", 0), ("X1", 1),
           ("XN2", 0), ("XN2", 1), ("FOUT", 0), ("FOUT", 1)]

PNBANKS = [("pp0", 6), ("pp1", 6), ("pp2", 4)]
SDBANKS = [("pa0", 6), ("pa1", 6), ("pb0", 4)]


def pn_slot(h):
    return PNBANKS[h // 6][0], 85 * (h % 6)


def sd_slot(h):
    return SDBANKS[h // 6][0], 85 * (h % 6)


def a1(ap):
    kk = ap.shape[-1]
    return ap.rearrange("p (a k) -> p a k", a=1)


def bc_inner(ap_2d, n_inner):
    P_, K_ = ap_2d.shape
    return ap_2d.rearrange("p (k a) -> p k a", a=1).broadcast_to([P_, K_, n_inner])




def bcast_row(nc, p, sb, src_row, dst, n, bank_name="pa0"):
    W = sb
    for c0 in range(0, n, 512):
        w = min(512, n - c0)
        t = p.pre("t")
        p.emit("t", t.matmul(W[bank_name][:, 0:w], W["ONESF"][:, :],
                             src_row[:, c0:c0 + w], start=True, stop=True,
                             skip_group_check=True))
        p.fence("t")
        v = p.pre("v")
        p.emit("v", v.tensor_copy(out=dst[:, c0:c0 + w], in_=W[bank_name][:, 0:w]))
        p.fence("v")


def prologue(nc, p, sb, L):
    W = sb
    for i in range(8):
        p.dma(out=W["WQKV"][:, 3072 * i:3072 * (i + 1)],
              in_=L["dwqkv"][128 * i:128 * (i + 1), :], queue="g")
        p.dma(out=W["WOUT"][:, 1024 * i:1024 * (i + 1)],
              in_=L["dwout"][128 * i:128 * (i + 1), :], queue="g")
    p.dma(out=W["GCOL"][:, :], in_=L["dg"].rearrange("(i p) -> p i", p=128), queue="s", allow_slow=True)
    p.dma(out=W["BCOL"][:, :], in_=L["db"].rearrange("(i p) -> p i", p=128), queue="s", allow_slow=True)
    p.dma(out=W["C64"][0:1, 0:16], in_=L["dbeta"].rearrange("(a h) -> a h", a=1), queue="s")
    p.dma(out=W["MASKA"][:, :], in_=L["dmask"], queue="s")
    p.dma(out=W["IDENT"][:, :], in_=L["dident"], queue="s")
    p.dma(out=W["IDENTF"][:, :], in_=L["didentf"], queue="s")
    p.dma(out=W["FTMP"][:, 0:32], in_=L["dch"], queue="s")
    p.dma(out=W["FTMP"][:, 32:64], in_=L["dcl"], queue="s")
    p.dma(out=W["FTMP"][:, 64:96], in_=L["dsh"], queue="s")
    p.dma(out=W["FTMP"][:, 96:128], in_=L["dsl"], queue="s")
    for i, (bn, half) in enumerate(RHSBUFS):
        p.dma(out=W[bn][:, 1024 * half:1024 * (half + 1)],
              in_=L["drhs"][128 * i:128 * (i + 1), :], queue="s")
    p.barrier_dma()

    v = p.pre("v")
    p.emit("v", v.tensor_tensor(out=W["COS2"][:, :], in0=W["FTMP"][:, 0:32],
                                in1=W["FTMP"][:, 32:64], op=ALU.add))
    p.emit("v", v.tensor_tensor(out=W["SIN2"][:, :], in0=W["FTMP"][:, 64:96],
                                in1=W["FTMP"][:, 96:128], op=ALU.add))
    p.emit("v", v.tensor_copy(out=W["C64B"][:, 0:8], in_=W["BCOL"][:, :]))
    p.emit("v", v.memset(W["ONESF"][:, :], 1.0))
    p.fence("v")
    a = p.pre("a")
    p.emit("a", a.activation(out=W["C64"][0:1, 16:32], in_=W["C64"][0:1, 0:16],
                             func=AF.Sigmoid))
    p.fence("a")
    bcast_row(nc, p, sb, W["C64"][0:1, 16:32], W["BETA"][:, :], 16)
    v = p.pre("v")
    p.emit("v", v.tensor_scalar(out=W["BETB"][:, :], in0=W["BETA"][:, :], scalar1=-1.0,
                                scalar2=1.0, op0=ALU.mult, op1=ALU.add))
    p.fence("v")

    t = p.pre("t")
    for j in range(16):
        for i in range(8):
            p.emit("t", t.matmul(W["pa0"][:, j:j + 1],
                                 W["WQKV"][:, 3072 * i + 128 * j:3072 * i + 128 * (j + 1)],
                                 W["C64B"][:, i:i + 1],
                                 start=(i == 0 and j == 0), stop=(i == 7),
                                 skip_group_check=True))
    for nch in range(2):
        bank = W["pa1"] if nch == 0 else W["pb0"]
        for i in range(8):
            c0 = 3072 * i + 2048 + 512 * nch
            p.emit("t", t.matmul(bank[0:1, 0:512],
                                 W["C64B"][:, i:i + 1], W["WQKV"][:, c0:c0 + 512],
                                 start=(i == 0), stop=(i == 7),
                                 skip_group_check=True))
    p.fence("t")
    v = p.pre("v")
    p.emit("v", v.tensor_copy(out=W["BIASF"][:, :], in_=W["pa0"][:, 0:16]))
    p.emit("v", v.tensor_copy(out=W["BVROWF"][:, 0:512], in_=W["pa1"][0:1, 0:512]))
    p.emit("v", v.tensor_copy(out=W["BVROWF"][:, 512:1024], in_=W["pb0"][0:1, 0:512]))
    for i in range(8):
        p.emit("v", v.tensor_scalar(out=W["WQKV"][:, 3072 * i:3072 * (i + 1)],
                                    in0=W["WQKV"][:, 3072 * i:3072 * (i + 1)],
                                    scalar1=W["GCOL"][:, i:i + 1], scalar2=None,
                                    op0=ALU.mult, op1=ALU.bypass))
    p.fence("v")
    bcast_row(nc, p, sb, W["BVROWF"][0:1, :], W["VBIASB"][:, :], 1024, bank_name="pa1")

    for i in range(8):
        bn, half = RHSBUFS[i]
        t = p.pre("t")
        for j in range(8):
            bank = W["pb0"] if j < 4 else W["pb1"]
            p.emit("t", t.transpose(bank[:, 128 * (j % 4):128 * (j % 4 + 1)],
                                    sb[bn][:, 1024 * half + 128 * j:1024 * half + 128 * (j + 1)],
                                    W["IDENTF"][:, :]))
        p.fence("t")
        v = p.pre("v")
        for j in range(8):
            bank = W["pb0"] if j < 4 else W["pb1"]
            p.emit("v", v.tensor_reduce(out=W["C64"][:, 48 + j:49 + j],
                                        in_=bank[:, 128 * (j % 4):128 * (j % 4 + 1)],
                                        axis=mybir.AxisListType.X, op=ALU.max,
                                        apply_absolute_value=True))
        v.drain()
        if i == 0:
            p.emit("v", v.tensor_copy(out=W["C64"][:, 32:40], in_=W["C64"][:, 48:56]))
        else:
            p.emit("v", v.tensor_tensor(out=W["C64"][:, 32:40], in0=W["C64"][:, 32:40],
                                        in1=W["C64"][:, 48:56], op=ALU.max))
        v.drain()
        p.fence("v")
    v = p.pre("v")
    p.emit("v", v.tensor_scalar(out=W["C64"][:, 32:40], in0=W["C64"][:, 32:40],
                                scalar1=1e-12, scalar2=None, op0=ALU.max, op1=ALU.bypass))
    p.fence("v")
    t = p.pre("t")
    p.emit("t", t.transpose(W["pb0"][0:8, 0:128], W["C64"][:, 32:40], W["IDENTF"][:, :]))
    p.fence("t")
    v = p.pre("v")
    p.emit("v", v.tensor_copy(out=W["ROWF"][0:8, 0:128], in_=W["pb0"][0:8, 0:128]))
    p.fence("v")
    for j in range(8):
        p.dma(out=W["RSB"][0:1, 128 * j:128 * (j + 1)],
              in_=W["ROWF"][j:j + 1, 0:128], queue="s")
    p.barrier_dma()
    v = p.pre("v")
    p.emit("v", v.reciprocal(out=W["INVSB"][0:1, 0:1024], in_=W["RSB"][0:1, 0:1024]))
    v.drain()
    p.emit("v", v.tensor_scalar(out=W["INVSB"][0:1, 0:1024], in0=W["INVSB"][0:1, 0:1024],
                                scalar1=127.0, scalar2=None, op0=ALU.mult, op1=ALU.bypass))
    p.emit("v", v.memset(W["C64"][:, 49:50], 127.0))
    v.drain()
    p.emit("v", v.reciprocal(out=W["C64"][:, 50:51], in_=W["C64"][:, 49:50]))
    v.drain()
    p.emit("v", v.tensor_scalar(out=W["RSB"][0:1, 0:1024], in0=W["RSB"][0:1, 0:1024],
                                scalar1=W["C64"][0:1, 50:51], scalar2=None,
                                op0=ALU.mult, op1=ALU.bypass))
    p.fence("v")
    bcast_row(nc, p, sb, W["INVSB"][0:1, 0:1024], W["INVSB"][:, :], 1024, bank_name="pa0")
    bcast_row(nc, p, sb, W["RSB"][0:1, 0:1024], W["RSB"][:, :], 1024, bank_name="pa1")
    for i in range(8):
        bn, half = RHSBUFS[i]
        src = sb[bn][:, 1024 * half:1024 * (half + 1)]
        quantize_tile(nc, p, sb, src_f32=src, invs=W["INVSB"][:, :],
                      dst=W["QRHS"][:, 1024 * i:1024 * (i + 1)], per_partition=False)

    v = p.pre("v")
    p.emit("v", v.memset(W["STF"][:, :], 0.0))
    p.emit("v", v.memset(W["STBP"][:, :], 0.0))
    p.emit("v", v.memset(W["UB"][:, :], 0.0))
    v.drain()
    stf3 = W["STF"].ap().rearrange("p (pr c) -> p pr c", c=65)
    p.emit("v", v.memset(stf3[:, :, 64:65], 1e-6))
    p.fence("v")
    state_cast(nc, p, sb)
    v = p.pre("v")
    vb4 = W["VB"].ap().rearrange("p (t2 h c) -> p t2 h c", t2=2, c=65)
    p.emit("v", v.memset(vb4[:, :, :, 64:65], 1.0))
    p.fence("v")


def state_cast(nc, p, sb):
    W = sb
    stf3 = W["STF"].ap().rearrange("p (pr c) -> p pr c", c=65)
    stbp2 = W["STBP"].ap().rearrange("p (m r) -> p m r", r=130)
    v = p.pre("v")
    p.emit("v", v.tensor_copy(out=stbp2[0:64, :, 0:65], in_=stf3[0:64, :, :]))
    p.emit("v", v.tensor_copy(out=stbp2[64:128, :, 65:130], in_=stf3[64:128, :, :]))
    p.fence("v")


def quantize_tile(nc, p, sb, src_f32, invs, dst, per_partition, invs_col=None):
    W = sb
    v = p.pre("v")
    v.drain()
    if per_partition:
        p.emit("v", v.tensor_scalar(out=W["FTMP"][:, :], in0=src_f32, scalar1=invs_col,
                                    scalar2=None, op0=ALU.mult, op1=ALU.bypass))
    else:
        p.emit("v", v.tensor_tensor(out=W["FTMP"][:, :], in0=src_f32, in1=invs,
                                    op=ALU.mult))
    p.fence("v")
    a = p.pre("a")
    p.emit("a", a.activation(out=W["QI"][:, 0:1024], in_=W["FTMP"][:, :], func=AF.Copy))
    p.fence("a")
    v = p.pre("v")
    p.emit("v", v.scalar_tensor_tensor(out=W["TQ"][:, 0:1024], in0=W["FTMP"][:, :],
                                       scalar=1.0, in1=W["QI"][:, 0:1024],
                                       op0=ALU.mult, op1=ALU.subtract))
    v.drain()
    p.emit("v", v.scalar_tensor_tensor(out=W["TMPA"][:, :], in0=W["TQ"][:, 0:1024],
                                       scalar=0.5, in1=W["QI"][:, 0:1024],
                                       op0=ALU.is_gt, op1=ALU.add))
    v.drain()
    p.emit("v", v.scalar_tensor_tensor(out=W["TMPB"][:, :], in0=W["TQ"][:, 0:1024],
                                       scalar=-0.5, in1=W["TMPA"][:, :],
                                       op0=ALU.is_ge, op1=ALU.add))
    v.drain()
    p.emit("v", v.tensor_scalar(out=W["TMPA"][:, :], in0=W["TMPB"][:, :],
                                scalar1=-126.0, scalar2=None, op0=ALU.max, op1=ALU.bypass))
    v.drain()
    p.emit("v", v.tensor_scalar(out=dst, in0=W["TMPA"][:, :], scalar1=128.0,
                                scalar2=-1.0, op0=ALU.min, op1=ALU.add))
    p.fence("v")


def ln_fused(nc, p, sb, src_ap, dst_ap, cbase, double):
    W = sb
    C = W["C64"]
    v = p.pre("v")
    v.drain()
    p.emit("v", v.bn_stats(out=C[:, cbase:cbase + 6], in_=src_ap[:, 0:512]))
    p.emit("v", v.bn_stats(out=C[:, cbase + 6:cbase + 12], in_=src_ap[:, 512:1024]))
    v.drain()
    p.emit("v", v.bn_aggr(out=C[:, cbase + 12:cbase + 14], in_=C[:, cbase:cbase + 12]))
    v.drain()
    m1 = C[:, cbase + 12:cbase + 13]
    v1 = C[:, cbase + 13:cbase + 14]
    A = C[:, cbase:cbase + 1]
    p.emit("v", v.tensor_scalar(out=A, in0=v1, scalar1=1e-5, scalar2=None,
                                op0=ALU.add, op1=ALU.bypass))
    v.drain()
    if double:
        RA = C[:, cbase + 1:cbase + 2]
        V2 = C[:, cbase + 2:cbase + 3]
        B = C[:, cbase + 3:cbase + 4]
        p.emit("v", v.reciprocal(out=RA, in_=A))
        v.drain()
        p.emit("v", v.tensor_tensor(out=V2, in0=v1, in1=RA, op=ALU.mult))
        v.drain()
        p.emit("v", v.tensor_scalar(out=B, in0=V2, scalar1=1e-5, scalar2=None,
                                    op0=ALU.add, op1=ALU.bypass))
        v.drain()
        p.emit("v", v.tensor_tensor(out=C[:, cbase + 4:cbase + 5], in0=A, in1=B,
                                    op=ALU.mult))
        p.fence("v")
        a = p.pre("a")
        p.emit("a", a.activation(out=C[:, cbase + 5:cbase + 6],
                                 in_=C[:, cbase + 4:cbase + 5], func=AF.Sqrt))
        p.fence("a")
    else:
        p.fence("v")
        a = p.pre("a")
        p.emit("a", a.activation(out=C[:, cbase + 5:cbase + 6], in_=A, func=AF.Sqrt))
        p.fence("a")
    v = p.pre("v")
    p.emit("v", v.reciprocal(out=C[:, cbase + 7:cbase + 8], in_=C[:, cbase + 5:cbase + 6]))
    v.drain()
    R = C[:, cbase + 7:cbase + 8]
    MR = C[:, cbase + 8:cbase + 9]
    p.emit("v", v.tensor_tensor(out=MR, in0=m1, in1=R, op=ALU.mult))
    v.drain()
    p.emit("v", v.tensor_scalar(out=C[:, cbase + 9:cbase + 10], in0=MR, scalar1=-1.0,
                                scalar2=None, op0=ALU.mult, op1=ALU.bypass))
    p.fence("v")
    if dst_ap is not None:
        a = p.pre("a")
        p.emit("a", a.activation(out=dst_ap, in_=src_ap, func=AF.Identity,
                                 bias=C[:, cbase + 9:cbase + 10], scale=R))
        p.fence("a")
    return R, C[:, cbase + 9:cbase + 10]


def chunk_stages(nc, p, sb, L, c):
    W = sb
    p.dma(out=W["XT"].ap().rearrange("p (a d) -> p a d", a=2),
          in_=L["dx"][CHUNK * c:CHUNK * (c + 1), :].rearrange("(a p) d -> p a d", p=128),
          queue="s")
    p.barrier_dma()

    for tt in range(NTT):
        ln_fused(nc, p, sb, W["XT"][:, 1024 * tt:1024 * (tt + 1)],
                 W["XN"][:, 1024 * tt:1024 * (tt + 1)], cbase=16 * tt, double=True)

    for tt in range(NTT):
        t = p.pre("t")
        for f in range(8):
            p.emit("t", t.transpose(W["ptr"][:, 128 * f:128 * (f + 1)],
                                    W["XN"][:, 1024 * tt + 128 * f:1024 * tt + 128 * (f + 1)],
                                    W["IDENT"][:, :]))
        p.fence("t")
        a = p.pre("a")
        p.emit("a", a.activation(
            out=W["XNT"].ap().rearrange("p (f b) -> p f b", b=256)[:, :, 128 * tt:128 * (tt + 1)],
            in_=W["ptr"][:, :], func=AF.Copy))
        p.fence("a")

    for j in range(16):
        bank = W["pa0"] if j % 2 == 0 else W["pa1"]
        col = 256 * ((j // 2) % 2)
        ps = bank[:, col:col + 256]
        t = p.pre("t")
        for i in range(8):
            p.emit("t", t.matmul(ps, W["WQKV"][:, 3072 * i + 128 * j:3072 * i + 128 * (j + 1)],
                                 W["XNT"][:, 256 * i:256 * (i + 1)],
                                 start=(i == 0), stop=(i == 7), skip_group_check=True))
        p.fence("t")
        a = p.pre("a")
        dstbuf = W["QN"] if j < 8 else W["KN"]
        pf = j % 8
        p.emit("a", a.activation(out=dstbuf[:, 256 * pf:256 * (pf + 1)], in_=ps,
                                 func=AF.Identity, bias=W["BIASF"][:, j:j + 1],
                                 scale=np.float32(1.0)))
        p.fence("a")

    for tt in range(NTT):
        for nch in range(2):
            ps = (W["pb0"] if nch == 0 else W["pb1"])[:, 0:512]
            t = p.pre("t")
            for i in range(8):
                c0 = 3072 * i + 2048 + 512 * nch
                p.emit("t", t.matmul(ps, W["XNT"][:, 256 * i + 128 * tt:256 * i + 128 * (tt + 1)],
                                     W["WQKV"][:, c0:c0 + 512],
                                     start=(i == 0), stop=(i == 7), skip_group_check=True))
            p.fence("t")
            v = p.pre("v")
            vb4 = W["VB"].ap().rearrange("p (t2 h cc) -> p t2 h cc", t2=2, cc=65)
            p.emit("v", v.tensor_tensor(
                out=vb4[:, tt, 8 * nch:8 * (nch + 1), 0:64],
                in0=ps.rearrange("p (h e) -> p h e", e=64),
                in1=W["VBIASB"][:, 512 * nch:512 * (nch + 1)].rearrange("p (h e) -> p h e", e=64),
                op=ALU.add))
            p.fence("v")

    cosb = a1(W["COS2"][:, :]).broadcast_to([128, 8, 32])
    sinb = a1(W["SIN2"][:, :]).broadcast_to([128, 8, 32])
    for srcb, dstb in (("QN", "QR"), ("KN", "KR")):
        for pf in range(8):
            src = W[srcb][:, 256 * pf:256 * (pf + 1)]
            v = p.pre("v")
            p.emit("v", v.tensor_copy(out=W["RT256"][0:32, :], in_=src[32:64, :]))
            p.emit("v", v.tensor_copy(out=W["RT256"][32:64, :], in_=src[0:32, :]))
            p.emit("v", v.tensor_copy(out=W["RT256"][64:96, :], in_=src[96:128, :]))
            p.emit("v", v.tensor_copy(out=W["RT256"][96:128, :], in_=src[64:96, :]))
            v.drain()
            p.emit("v", v.tensor_tensor(out=W["TMPA"][:, 0:256].rearrange("p (r k) -> p r k", k=32),
                                        in0=src.rearrange("p (r k) -> p r k", k=32),
                                        in1=cosb, op=ALU.mult))
            p.emit("v", v.tensor_tensor(out=W["TMPB"][:, 0:256].rearrange("p (r k) -> p r k", k=32),
                                        in0=W["RT256"][:, :].rearrange("p (r k) -> p r k", k=32),
                                        in1=sinb, op=ALU.mult))
            v.drain()
            p.emit("v", v.tensor_tensor(out=W[dstb][:, 256 * pf:256 * (pf + 1)],
                                        in0=W["TMPA"][:, 0:256], in1=W["TMPB"][:, 0:256],
                                        op=ALU.add))
            p.fence("v")

    for srcb, dstb in (("QN", "SQ"), ("KN", "SK")):
        for pf in range(8):
            src = W[srcb][:, 256 * pf:256 * (pf + 1)]
            g = p.pre("g")
            p.emit("g", g.tensor_scalar(out=W["TMPA"][:, 0:256], in0=src, scalar1=0.0,
                                        scalar2=None, op0=ALU.min, op1=ALU.bypass))
            p.fence("g")
            a = p.pre("a")
            p.emit("a", a.activation(out=W["TMPB"][:, 0:256], in_=W["TMPA"][:, 0:256],
                                     func=AF.Exp))
            p.fence("a")
            v = p.pre("v")
            p.emit("v", v.scalar_tensor_tensor(out=W[dstb][:, 256 * pf:256 * (pf + 1)],
                                               in0=src, scalar=0.0, in1=W["TMPB"][:, 0:256],
                                               op0=ALU.max, op1=ALU.add))
            p.fence("v")

    for tt in range(NTT):
        t = p.pre("t")
        for pf in range(8):
            p.emit("t", t.transpose(W["ptr"][:, 128 * pf:128 * (pf + 1)],
                                    W["SK"][:, 256 * pf + 128 * tt:256 * pf + 128 * (tt + 1)],
                                    W["IDENT"][:, :]))
        p.fence("t")
        a = p.pre("a")
        p.emit("a", a.activation(out=W["SKT"][:, 1024 * tt:1024 * (tt + 1)],
                                 in_=W["ptr"][:, :], func=AF.Copy))
        p.fence("a")

    for bl in range(NBL):
        attn_block(nc, p, sb, L, c, bl)

    for tt in range(NTT):
        t = p.pre("t")
        for f in range(8):
            p.emit("t", t.transpose(W["ptr"][:, 128 * f:128 * (f + 1)],
                                    W["OB"][:, 1024 * tt + 128 * f:1024 * tt + 128 * (f + 1)],
                                    W["IDENT"][:, :]))
        p.fence("t")
        a = p.pre("a")
        p.emit("a", a.activation(
            out=W["OT"].ap().rearrange("p (f b) -> p f b", b=256)[:, :, 128 * tt:128 * (tt + 1)],
            in_=W["ptr"][:, :], func=AF.Copy))
        p.fence("a")
    for tt in range(NTT):
        for nch in range(2):
            ps = (W["pb0"] if nch == 0 else W["pb1"])[:, 0:512]
            t = p.pre("t")
            for i in range(8):
                p.emit("t", t.matmul(ps, W["OT"][:, 256 * i + 128 * tt:256 * i + 128 * (tt + 1)],
                                     W["WOUT"][:, 1024 * i + 512 * nch:1024 * i + 512 * (nch + 1)],
                                     start=(i == 0), stop=(i == 7), skip_group_check=True))
            p.fence("t")
            v = p.pre("v")
            p.emit("v", v.tensor_copy(out=W["FTMP"][:, 512 * nch:512 * (nch + 1)], in_=ps))
            p.fence("v")
        R, NMR = ln_fused(nc, p, sb, W["FTMP"][:, 0:1024], None, cbase=32, double=False)
        a = p.pre("a")
        p.emit("a", a.activation(out=W["TMPA"][:, :], in_=W["FTMP"][:, 0:1024],
                                 func=AF.Identity, bias=NMR, scale=R))
        p.fence("a")
        v = p.pre("v")
        p.emit("v", v.tensor_tensor(out=W["X1"][:, 1024 * tt:1024 * (tt + 1)],
                                    in0=W["XT"][:, 1024 * tt:1024 * (tt + 1)],
                                    in1=W["TMPA"][:, :], op=ALU.add))
        p.fence("v")

    for tt in range(NTT):
        ln_fused(nc, p, sb, W["X1"][:, 1024 * tt:1024 * (tt + 1)],
                 W["XN2"][:, 1024 * tt:1024 * (tt + 1)], cbase=32, double=False)
        v = p.pre("v")
        p.emit("v", v.tensor_reduce(out=W["C64"][:, 44:45], in_=W["XN2"][:, 1024 * tt:1024 * (tt + 1)],
                                    axis=mybir.AxisListType.X, op=ALU.max,
                                    apply_absolute_value=True))
        v.drain()
        p.emit("v", v.tensor_scalar(out=W["C64"][:, 45:46], in0=W["C64"][:, 44:45],
                                    scalar1=1e-12, scalar2=None, op0=ALU.max, op1=ALU.bypass))
        v.drain()
        p.emit("v", v.reciprocal(out=W["C64"][:, 46:47], in_=W["C64"][:, 45:46]))
        v.drain()
        p.emit("v", v.tensor_scalar(out=W["C64"][:, 46:47], in0=W["C64"][:, 46:47],
                                    scalar1=127.0, scalar2=None, op0=ALU.mult, op1=ALU.bypass))
        p.emit("v", v.tensor_scalar(out=W["C64"][:, 47:48], in0=W["C64"][:, 45:46],
                                    scalar1=W["C64"][:, 50:51], scalar2=None,
                                    op0=ALU.mult, op1=ALU.bypass))
        p.fence("v")
        quantize_tile(nc, p, sb, src_f32=W["XN2"][:, 1024 * tt:1024 * (tt + 1)], invs=None,
                      dst=W["QLTOK"][:, 1024 * tt:1024 * (tt + 1)], per_partition=True,
                      invs_col=W["C64"][:, 46:47])
        t = p.pre("t")
        for f in range(8):
            p.emit("t", t.transpose(W["ptr"][:, 128 * f:128 * (f + 1)],
                                    W["QLTOK"][:, 1024 * tt + 128 * f:1024 * tt + 128 * (f + 1)],
                                    W["IDENT"][:, :]))
        p.fence("t")
        a = p.pre("a")
        p.emit("a", a.activation(
            out=W["QLT"].ap().rearrange("p (f b) -> p f b", b=256)[:, :, 128 * tt:128 * (tt + 1)],
            in_=W["ptr"][:, :], func=AF.Copy))
        p.fence("a")
        for nch in range(2):
            ps = (W["pb0"] if nch == 0 else W["pb1"])[:, 0:512]
            t = p.pre("t")
            for i in range(8):
                p.emit("t", t.matmul(ps, W["QLT"][:, 256 * i + 128 * tt:256 * i + 128 * (tt + 1)],
                                     W["QRHS"][:, 1024 * i + 512 * nch:1024 * i + 512 * (nch + 1)],
                                     start=(i == 0), stop=(i == 7), skip_group_check=True))
            p.fence("t")
            v = p.pre("v")
            p.emit("v", v.tensor_scalar(out=W["FTMP"][:, 512 * nch:512 * (nch + 1)], in0=ps,
                                        scalar1=W["C64"][:, 47:48], scalar2=None,
                                        op0=ALU.mult, op1=ALU.bypass))
            v.drain()
            p.emit("v", v.tensor_tensor(out=W["FTMP"][:, 512 * nch:512 * (nch + 1)],
                                        in0=W["FTMP"][:, 512 * nch:512 * (nch + 1)],
                                        in1=W["RSB"][:, 512 * nch:512 * (nch + 1)], op=ALU.mult))
            p.fence("v")
        R, NMR = ln_fused(nc, p, sb, W["FTMP"][:, 0:1024], None, cbase=32, double=False)
        a = p.pre("a")
        p.emit("a", a.activation(out=W["TMPA"][:, :], in_=W["FTMP"][:, 0:1024],
                                 func=AF.Identity, bias=NMR, scale=R))
        p.fence("a")
        v = p.pre("v")
        p.emit("v", v.tensor_tensor(out=W["FOUT"][:, 1024 * tt:1024 * (tt + 1)],
                                    in0=W["X1"][:, 1024 * tt:1024 * (tt + 1)],
                                    in1=W["TMPA"][:, :], op=ALU.add))
        p.fence("v")
    p.dma(out=L["dout"][CHUNK * c:CHUNK * (c + 1), :].rearrange("(a p) d -> p a d", p=128),
          in_=W["FOUT"].ap().rearrange("p (a d) -> p a d", a=2), queue="s")

    if c == 0 and L["debug_dump"]:
        p.barrier_dma()
        for name, buf, shape, dt in L["debug_dump"]:
            if name in ("PSA", "PSB"):
                continue
            p.dma(out=L["ddbg"][name], in_=W[buf][:, :], queue="s")
        p.barrier_dma()


def attn_block(nc, p, sb, L, c, bl):
    W = sb
    maskb = a1(W["MASKA"][:, :]).broadcast_to([128, 4, 128])
    eb4 = W["EB"].ap().rearrange("p (w j two q) -> p w j two q", w=2, j=4, two=2)
    for wv in range(2):
        t = p.pre("t")
        for h in range(8 * wv, 8 * wv + 8):
            pf, half = h // 2, h % 2
            rows = slice(64 * half, 64 * (half + 1))
            bank = W["pa0"] if half == 0 else W["pa1"]
            slot = (h // 2) % 4
            tok = slice(256 * pf + 128 * bl, 256 * pf + 128 * (bl + 1))
            p.emit("t", t.matmul(bank[:, 128 * slot:128 * (slot + 1)],
                                 W["KR"][rows, tok], W["QR"][rows, tok],
                                 start=True, stop=True, skip_group_check=True))
        p.fence("t")
        v = p.pre("v")
        p.emit("v", v.tensor_tensor(out=W["FTMP"][:, 0:512].rearrange("p (s q) -> p s q", q=128),
                                    in0=W["pa0"].ap().rearrange("p (s q) -> p s q", q=128),
                                    in1=maskb, op=ALU.add))
        p.emit("v", v.tensor_tensor(out=W["FTMP"][:, 512:1024].rearrange("p (s q) -> p s q", q=128),
                                    in0=W["pa1"].ap().rearrange("p (s q) -> p s q", q=128),
                                    in1=maskb, op=ALU.add))
        p.fence("v")
        a = p.pre("a")
        p.emit("a", a.activation(out=eb4[:, wv, :, 0, :],
                                 in_=W["FTMP"][:, 0:512], func=AF.Exp, scale=np.float32(0.125)))
        p.emit("a", a.activation(out=eb4[:, wv, :, 1, :],
                                 in_=W["FTMP"][:, 512:1024], func=AF.Exp, scale=np.float32(0.125)))
        p.fence("a")

    ghb4 = W["GHB"].ap().rearrange("p (w j two q) -> p w j two q", w=4, j=2, two=2)
    for wv in range(4):
        t = p.pre("t")
        for h in range(4 * wv, 4 * wv + 4):
            pf, half = h // 2, h % 2
            rows = slice(64 * half, 64 * (half + 1))
            bank = W["pb0"] if half == 0 else W["pb1"]
            slot = (h // 2) % 2
            tok = slice(256 * pf + 128 * bl, 256 * pf + 128 * (bl + 1))
            p.emit("t", t.matmul(bank[:, 256 * slot:256 * slot + 128],
                                 W["SK"][rows, tok], W["SQ"][rows, tok],
                                 start=True, stop=True, skip_group_check=True))
            p.emit("t", t.matmul(bank[:, 256 * slot + 128:256 * slot + 256],
                                 W["SK"][rows, tok], W["SK"][rows, tok],
                                 start=True, stop=True, skip_group_check=True))
        p.fence("t")
        a = p.pre("a")
        p.emit("a", a.activation(out=ghb4[:, wv, :, 0, :], in_=W["pb0"][:, :], func=AF.Copy))
        p.emit("a", a.activation(out=ghb4[:, wv, :, 1, :], in_=W["pb1"][:, :], func=AF.Copy))
        p.fence("a")

    t = p.pre("t")
    for h in range(H):
        pf = h // 2
        tok = slice(256 * pf + 128 * bl, 256 * pf + 128 * (bl + 1))
        bank, colh = pn_slot(h)
        p.emit("t", t.matmul(W[bank][:, colh:colh + 65],
                             W["SK"][:, tok], W["STBP"][:, 65 * h:65 * (h + 1)],
                             start=(h % 6 == 0), stop=True, skip_group_check=True))
    p.fence("t")

    t = p.pre("t")
    for h in range(H):
        pf = h // 2
        tok = slice(256 * pf + 128 * bl, 256 * pf + 128 * (bl + 1))
        bank, colh = sd_slot(h)
        p.emit("t", t.matmul(W[bank][:, colh:colh + 65],
                             W["SQ"][:, tok], W["STBP"][:, 65 * h:65 * (h + 1)],
                             start=(h % 6 == 0), stop=True, skip_group_check=True))
    p.fence("t")

    v = p.pre("v")
    p.emit("v", v.memset(W["UB"][:, :], 0.0))
    p.fence("v")
    vb4 = W["VB"].ap().rearrange("p (t2 h cc) -> p t2 h cc", t2=2, cc=65)
    ub3 = W["UB"].ap().rearrange("p (h cc) -> p h cc", cc=65)
    tb3f = W["TBF"].ap().rearrange("p (h cc) -> p h cc", cc=65)
    for s in range(4):
        rows = slice(32 * s, 32 * (s + 1))
        if s > 0:
            t = p.pre("t")
            for h in range(H):
                bank, colh = pn_slot(h)
                p.emit("t", t.matmul(W[bank][rows, colh:colh + 65],
                                     W["GHB"][:, 256 * h + 128 + 32 * s:256 * h + 128 + 32 * (s + 1)],
                                     W["UB"][:, 65 * h:65 * (h + 1)],
                                     start=False, stop=True, tile_position=(0, 32 * s),
                                     skip_group_check=True))
            for h in range(H):
                bank, colh = sd_slot(h)
                p.emit("t", t.matmul(W[bank][rows, colh:colh + 65],
                                     W["GHB"][:, 256 * h + 32 * s:256 * h + 32 * (s + 1)],
                                     W["UB"][:, 65 * h:65 * (h + 1)],
                                     start=False, stop=True, tile_position=(0, 32 * s),
                                     skip_group_check=True))
            p.fence("t")
        v = p.pre("v")
        for bi, (bank, nh) in enumerate(PNBANKS):
            dens = W[bank][:, 0:85 * nh].rearrange("p (hh e) -> p hh e", e=85)[rows, 0:nh, 64:65]
            p.emit("v", v.reciprocal(out=bc_inner(W["RUF"][rows, 6 * bi:6 * bi + nh], 1),
                                     in_=dens))
        v.drain()
        p.emit("v", v.tensor_copy(out=W["RUB"][rows, :], in_=W["RUF"][rows, :]))
        v.drain()
        for bi, (bank, nh) in enumerate(PNBANKS):
            nums = W[bank][:, 0:85 * nh].rearrange("p (hh e) -> p hh e", e=85)[rows, 0:nh, 0:64]
            p.emit("v", v.tensor_tensor(out=tb3f[rows, 6 * bi:6 * bi + nh, 0:64], in0=nums,
                                        in1=bc_inner(W["RUB"][rows, 6 * bi:6 * bi + nh], 64),
                                        op=ALU.mult))
        v.drain()
        p.emit("v", v.tensor_tensor(out=ub3[rows, :, 0:64], in0=vb4[rows, bl, :, 0:64],
                                    in1=tb3f[rows, :, 0:64], op=ALU.subtract))
        v.drain()
        p.emit("v", v.memset(ub3[rows, :, 64:65], 1.0))
        p.fence("v")

    t = p.pre("t")
    for h in range(H):
        bank, colh = pn_slot(h)
        p.emit("t", t.matmul(W[bank][:, colh:colh + 65],
                             W["EB"][:, 128 * h:128 * (h + 1)],
                             W["VB"][:, 1040 * bl + 65 * h:1040 * bl + 65 * (h + 1)],
                             start=True, stop=True, skip_group_check=True))
    p.fence("t")

    v = p.pre("v")
    for bi, (bank, nh) in enumerate(SDBANKS):
        dens = W[bank][:, 0:85 * nh].rearrange("p (hh e) -> p hh e", e=85)[:, 0:nh, 64:65]
        p.emit("v", v.tensor_scalar(out=bc_inner(W["RQF"][:, 6 * bi:6 * bi + nh], 1),
                                    in0=dens, scalar1=1e-6, scalar2=None,
                                    op0=ALU.add, op1=ALU.bypass))
    v.drain()
    p.emit("v", v.reciprocal(out=W["RUF"][:, 0:16], in_=W["RQF"][:, 0:16]))
    v.drain()
    p.emit("v", v.tensor_tensor(out=W["RQF"][:, 0:16], in0=W["RUF"][:, 0:16],
                                in1=W["BETA"][:, :], op=ALU.mult))
    v.drain()
    for bi, (bank, nh) in enumerate(PNBANKS):
        dens = W[bank][:, 0:85 * nh].rearrange("p (hh e) -> p hh e", e=85)[:, 0:nh, 64:65]
        p.emit("v", v.reciprocal(out=bc_inner(W["RUF"][:, 6 * bi:6 * bi + nh], 1), in_=dens))
    v.drain()
    p.emit("v", v.tensor_tensor(out=W["RUF"][:, 0:16], in0=W["RUF"][:, 0:16],
                                in1=W["BETB"][:, :], op=ALU.mult))
    v.drain()
    p.emit("v", v.tensor_copy(out=W["RQB"][:, 0:16], in_=W["RQF"][:, 0:16]))
    p.emit("v", v.tensor_copy(out=W["RSBC"][:, 0:16], in_=W["RUF"][:, 0:16]))
    v.drain()
    tm3a = W["TMPA"].ap().rearrange("p (hh e) -> p hh e", e=64)
    tm3b = W["TMPB"].ap().rearrange("p (hh e) -> p hh e", e=64)
    for bi, (bank, nh) in enumerate(SDBANKS):
        nums = W[bank][:, 0:85 * nh].rearrange("p (hh e) -> p hh e", e=85)[:, 0:nh, 0:64]
        p.emit("v", v.tensor_tensor(out=tm3a[:, 6 * bi:6 * bi + nh, :], in0=nums,
                                    in1=bc_inner(W["RQB"][:, 6 * bi:6 * bi + nh], 64),
                                    op=ALU.mult))
    for bi, (bank, nh) in enumerate(PNBANKS):
        nums = W[bank][:, 0:85 * nh].rearrange("p (hh e) -> p hh e", e=85)[:, 0:nh, 0:64]
        p.emit("v", v.tensor_tensor(out=tm3b[:, 6 * bi:6 * bi + nh, :], in0=nums,
                                    in1=bc_inner(W["RSBC"][:, 6 * bi:6 * bi + nh], 64),
                                    op=ALU.mult))
    v.drain()
    p.emit("v", v.tensor_tensor(out=W["OB"][:, 1024 * bl:1024 * (bl + 1)],
                                in0=W["TMPA"][:, :], in1=W["TMPB"][:, :], op=ALU.add))
    p.fence("v")

    if c == 0 and bl == 1 and L.get("dump_comb"):
        p.dma(out=L["ddbg"]["TMA"], in_=W["TMPA"][:, :], queue="s")
        p.dma(out=L["ddbg"]["TMB"], in_=W["TMPB"][:, :], queue="s")
        p.dma(out=L["ddbg"]["RQBd"], in_=W["RQB"][:, :], queue="s")
        p.dma(out=L["ddbg"]["RSCd"], in_=W["RSBC"][:, :], queue="s")
        p.barrier_dma()
    if c == 0 and bl == 1 and L.get("dump_psum"):
        v = p.pre("v")
        for di, (bk, nh) in enumerate(SDBANKS):
            src_ = W[bk][:, 0:85 * nh].rearrange("p (hh e) -> p hh e", e=85)[:, :, 0:65]
            dst_ = W["XN2"][:, 390 * di:390 * di + 65 * nh].rearrange("p (hh e) -> p hh e", e=65)
            p.emit("v", v.tensor_copy(out=dst_, in_=src_))
        for di, (bk, nh) in enumerate(PNBANKS):
            src_ = W[bk][:, 0:85 * nh].rearrange("p (hh e) -> p hh e", e=85)[:, :, 0:65]
            dst_ = W["FOUT"][:, 390 * di:390 * di + 65 * nh].rearrange("p (hh e) -> p hh e", e=65)
            p.emit("v", v.tensor_copy(out=dst_, in_=src_))
        p.emit("v", v.tensor_copy(out=W["FOUT"][:, 1536:1552], in_=W["RQF"][:, :]))
        p.emit("v", v.tensor_copy(out=W["FOUT"][:, 1552:1568], in_=W["RUF"][:, :]))
        p.fence("v")
        p.dma(out=L["ddbg"]["PSA"], in_=W["XN2"][:, :], queue="s")
        p.dma(out=L["ddbg"]["PSB"], in_=W["FOUT"][:, :], queue="s")
        p.barrier_dma()

    t = p.pre("t")
    for h in range(H):
        pr, half = h // 2, h % 2
        bank = W["pb1"] if pr < 4 else W["pb0"]
        colp = 128 * (pr % 4)
        p.emit("t", t.matmul(bank[64 * half:64 * (half + 1), colp:colp + 65],
                             W["SKT"][:, 1024 * bl + 128 * pr + 64 * half:1024 * bl + 128 * pr + 64 * (half + 1)],
                             W["UB"][:, 65 * h:65 * (h + 1)],
                             start=True, stop=True, tile_position=(0, 64 * half),
                             skip_group_check=True))
    p.fence("t")
    v = p.pre("v")
    for bi, bank in enumerate((W["pb1"], W["pb0"])):
        src = bank.ap().rearrange("p (pr e) -> p pr e", e=128)[:, 0:4, 0:65]
        dst = W["STF"].ap().rearrange("p (pr e) -> p pr e", e=65)[:, 4 * bi:4 * bi + 4, :]
        p.emit("v", v.tensor_tensor(out=dst, in0=src, in1=dst, op=ALU.add))
    p.fence("v")
    state_cast(nc, p, sb)


def epilogue(nc, p, sb, L):
    s = p.pre("s")
    s.wait_ge(p.dsem, 16 * p.dcnt)
    if p.dcnt_g:
        s.wait_ge(p.dsem_g, 16 * p.dcnt_g)


_NC_CACHE = {}


def kernel(L0_attn_x, L0_attn_w_qkv, L0_attn_w_out, L0_attn_ln_g, L0_attn_ln_b,
           L0_attn_mem_beta, L0_ffn_rhs):
    from concourse.bass_utils import run_bass_kernel_spmd
    x = np.asarray(L0_attn_x, dtype=np.float32)
    base = make_inputs(x[0], L0_attn_w_qkv, L0_attn_w_out, L0_attn_ln_g,
                       L0_attn_ln_b, L0_attn_mem_beta, L0_ffn_rhs)
    in_maps = []
    for bidx in range(8):
        m = dict(base)
        m["x"] = np.ascontiguousarray(x[bidx])
        in_maps.append(m)
    if "nc" not in _NC_CACHE:
        _NC_CACHE["nc"] = build_core(debug_dump=[], n_chunks=NCH)
    res = run_bass_kernel_spmd(_NC_CACHE["nc"], in_maps, core_ids=list(range(8)))
    out = np.stack([np.asarray(r["out"], dtype=np.float32) for r in res.results], axis=0)
    return out


# version 02
# speedup vs baseline: 1.0931x; 1.0931x over previous
import numpy as np
import contextlib
import concourse.bass as bass
import concourse.mybir as mybir

F32 = mybir.dt.float32
I32 = mybir.dt.int32
BF16 = mybir.dt.bfloat16

H, DH, SEG, D, N = 16, 64, 32, 1024, 4096
CHUNK, NCH = 256, 16


def host_constants():
    c = {}
    perm = np.concatenate([np.arange(0, 64, 2), np.arange(1, 64, 2)])
    c["perm_head"] = perm
    pos = np.arange(SEG, dtype=np.float64)
    freqs = 10000.0 ** (-np.arange(0, 64, 2, dtype=np.float64) / 64)
    ang = freqs[:, None] * pos[None, :]
    cos = np.cos(ang)
    sin = np.sin(ang)
    cos64 = np.concatenate([cos, cos], 0)
    sin64 = np.concatenate([-sin, sin], 0)
    cos128 = np.concatenate([cos64, cos64], 0)
    sin128 = np.concatenate([sin64, sin64], 0)
    c["cos_hi"] = np.ascontiguousarray(cos128.astype(np.float32))
    c["cos_lo"] = np.ascontiguousarray((cos128 - c["cos_hi"].astype(np.float64)).astype(np.float32))
    c["sin_hi"] = np.ascontiguousarray(sin128.astype(np.float32))
    c["sin_lo"] = np.ascontiguousarray((sin128 - c["sin_hi"].astype(np.float64)).astype(np.float32))
    kt = np.arange(128)[:, None]
    qt = np.arange(128)[None, :]
    allowed = (kt // SEG == qt // SEG) & (kt <= qt)
    c["maska"] = np.ascontiguousarray(np.where(allowed, 0.0, -1e30).astype(np.float32))
    import ml_dtypes
    c["ident"] = np.ascontiguousarray(np.eye(128, dtype=np.float32).astype(ml_dtypes.bfloat16))
    c["identf"] = np.ascontiguousarray(np.eye(128, dtype=np.float32))
    return c


def permute_wqkv(w_qkv):
    perm = host_constants()["perm_head"]
    cols = np.arange(3072)
    for h in range(H):
        cols[64 * h: 64 * h + 64] = 64 * h + perm
        cols[1024 + 64 * h: 1024 + 64 * h + 64] = 1024 + 64 * h + perm
    return np.ascontiguousarray(w_qkv[:, cols])


class P:
    ENGS = ("t", "v", "a", "g", "s")

    def __init__(self, nc, stack):
        self.nc = nc
        self.sem = {e: stack.enter_context(nc.semaphore("sem_" + e)) for e in self.ENGS}
        self.dsem = stack.enter_context(nc.semaphore("sem_dma"))
        self.dsem_g = stack.enter_context(nc.semaphore("sem_dmag"))
        self.dcnt_g = 0
        self.cnt = {e: 0 for e in self.ENGS}
        self.dcnt = 0
        self.eng = {"t": nc.tensor, "v": nc.vector, "a": nc.scalar,
                    "g": nc.gpsimd, "s": nc.sync}
        self.last = {e: None for e in self.ENGS}
        self.need_wait = {e: {} for e in self.ENGS}

    def emit(self, e, instr):
        if instr is not None:
            self.last[e] = instr
        return instr

    def _flush_waits(self, e):
        nw = self.need_wait[e]
        for o, cval in sorted(nw.items()):
            if o == "d":
                self.eng[e].wait_ge(self.dsem, 16 * cval)
            elif o == "dg":
                self.eng[e].wait_ge(self.dsem_g, 16 * cval)
            else:
                self.eng[e].wait_ge(self.sem[o], cval)
        self.need_wait[e] = {}

    def pre(self, e):
        self._flush_waits(e)
        return self.eng[e]

    def fence(self, *engines, waiters=None):
        for e in engines:
            if self.last[e] is not None:
                self.last[e].then_inc(self.sem[e], 1)
                self.cnt[e] += 1
                self.last[e] = None
        for w in (self.ENGS if waiters is None else waiters):
            for e in engines:
                self.need_wait[w][e] = self.cnt[e]
        return self.cnt[engines[0]]

    def mini(self, e):
        if self.last[e] is None:
            return self.cnt[e]
        self.last[e].then_inc(self.sem[e], 1)
        self.cnt[e] += 1
        self.last[e] = None
        self.eng[e].wait_ge(self.sem[e], self.cnt[e])
        return self.cnt[e]

    def wait_cnt(self, w, e, cval):
        if cval > 0:
            cur = self.need_wait[w].get(e, 0)
            self.need_wait[w][e] = max(cur, cval)

    def dma(self, out, in_, queue="s", allow_slow=False):
        eng = self.pre(queue)
        instr = eng.dma_start(out=out, in_=in_, allow_slow_non_contiguous=allow_slow)
        if queue == "g":
            instr.then_inc(self.dsem_g, 16)
            self.dcnt_g += 1
        else:
            instr.then_inc(self.dsem, 16)
            self.dcnt += 1
        return instr

    def barrier_dma(self):
        for w in self.ENGS:
            self.need_wait[w]["d"] = self.dcnt
            if self.dcnt_g:
                self.need_wait[w]["dg"] = self.dcnt_g


BUFS = [
    ("WQKV", (128, 8 * 3072), BF16),
    ("WOUT", (128, 8 * 1024), BF16),
    ("QRHS", (128, 8 * 1024), BF16),
    ("RSB", (128, 1024), F32),
    ("INVSB", (128, 1024), F32),
    ("VBIASB", (128, 1024), F32),
    ("GCOL", (128, 8), F32),
    ("BCOL", (128, 8), F32),
    ("BIASF", (128, 16), F32),
    ("BVROWF", (1, 1024), F32),
    ("ONER", (1, 128), BF16),
    ("ONESF", (1, 128), F32),
    ("ONESH", (128, 128), BF16),
    ("VBIASH", (128, 1024), BF16),
    ("BETA", (128, 16), F32),
    ("BETB", (128, 16), F32),
    ("COS2", (128, 32), BF16),
    ("SIN2", (128, 32), BF16),
    ("MASKA", (128, 128), F32),
    ("IDENT", (128, 128), BF16),
    ("IDENTF", (128, 128), F32),
    ("STF", (128, 520), F32),
    ("STBP", (128, 1040), BF16),
    ("C64", (128, 64), F32),
    ("C64B", (128, 64), BF16),
    ("RUF", (128, 16), F32),
    ("RQF", (128, 16), F32),
    ("XT", (128, 2048), F32),
    ("X1", (128, 2048), F32),
    ("XN2", (128, 2048), F32),
    ("FOUT", (128, 2048), F32),
    ("XN", (128, 2048), BF16),
    ("XNT", (128, 2048), BF16),
    ("QN", (128, 2048), BF16),
    ("KN", (128, 2048), BF16),
    ("QR", (128, 2048), BF16),
    ("KR", (128, 2048), BF16),
    ("SQ", (128, 2048), BF16),
    ("SK", (128, 2048), BF16),
    ("VB", (128, 2080), BF16),
    ("GHB", (128, 4096), BF16),
    ("UB", (128, 1040), BF16),
    ("TBF", (128, 1040), BF16),
    ("OT", (128, 2048), BF16),
    ("QI", (128, 1024), I32),
    ("TQ", (128, 1024), BF16),
    ("TMPA", (128, 1024), BF16),
    ("TMPB", (128, 1024), BF16),
    ("FTMP", (128, 1024), F32),
    ("ROWF", (8, 1024), F32),
]
ALIASES = {"QLTOK": "XN", "QLT": "OT", "ROWS": "FTMP", "JUNK": "TQ",
           "OB": "XNT", "SKT": "KN", "EB": "QN"}


def build_core(debug_dump=(), n_chunks=NCH):
    nc = bass.Bass("TRN2", target_bir_lowering=False)
    L = {}
    L["dx"] = nc.dram_tensor("x", [N, D], F32, kind="ExternalInput").ap()
    L["dwqkv"] = nc.dram_tensor("wqkv", [D, 3072], F32, kind="ExternalInput").ap()
    L["dwout"] = nc.dram_tensor("wout", [D, D], F32, kind="ExternalInput").ap()
    L["dg"] = nc.dram_tensor("ln_g", [D], F32, kind="ExternalInput").ap()
    L["db"] = nc.dram_tensor("ln_b", [D], F32, kind="ExternalInput").ap()
    L["dbeta"] = nc.dram_tensor("mem_beta", [H], F32, kind="ExternalInput").ap()
    L["drhs"] = nc.dram_tensor("rhs", [D, D], F32, kind="ExternalInput").ap()
    L["dch"] = nc.dram_tensor("cos_hi", [128, 32], F32, kind="ExternalInput").ap()
    L["dcl"] = nc.dram_tensor("cos_lo", [128, 32], F32, kind="ExternalInput").ap()
    L["dsh"] = nc.dram_tensor("sin_hi", [128, 32], F32, kind="ExternalInput").ap()
    L["dsl"] = nc.dram_tensor("sin_lo", [128, 32], F32, kind="ExternalInput").ap()
    L["dmask"] = nc.dram_tensor("maska", [128, 128], F32, kind="ExternalInput").ap()
    L["dident"] = nc.dram_tensor("ident", [128, 128], BF16, kind="ExternalInput").ap()
    L["didentf"] = nc.dram_tensor("identf", [128, 128], F32, kind="ExternalInput").ap()
    L["dout"] = nc.dram_tensor("out", [N, D], F32, kind="ExternalOutput").ap()
    L["ddbg"] = {}
    L["debug_dump"] = list(debug_dump)
    L["n_chunks"] = n_chunks
    L["dump_psum"] = any(n == "PSA" for n, _, _, _ in debug_dump)
    L["dump_comb"] = any(n == "TMA" for n, _, _, _ in debug_dump)
    for name, buf, shape, dt in debug_dump:
        L["ddbg"][name] = nc.dram_tensor("dbg_" + name, list(shape), dt,
                                         kind="ExternalOutput").ap()

    with contextlib.ExitStack() as st_:
        sb = {}
        for name, shape, dt in BUFS:
            sb[name] = st_.enter_context(nc.sbuf_tensor(name, list(shape), dt))
        for al, tgt in ALIASES.items():
            sb[al] = sb[tgt]
        for nm in ("pa0", "pa1", "pb0", "pb1", "pp0", "pp1", "pp2"):
            sb[nm] = st_.enter_context(nc.psum_tensor(nm, [128, 512], F32))
        sb["ptr"] = st_.enter_context(nc.psum_tensor("ptr", [128, 1024], BF16))
        p = P(nc, st_)
        with nc.Block() as block:
            @block.sync
            def _(sync):
                prologue(nc, p, sb, L)
                for c in range(n_chunks):
                    chunk_stages(nc, p, sb, L, c)
                epilogue(nc, p, sb, L)
    return nc


def make_inputs(x_b, w_qkv, w_out, ln_g, ln_b, mem_beta, rhs):
    c = host_constants()
    return {
        "x": np.ascontiguousarray(x_b.astype(np.float32)),
        "wqkv": permute_wqkv(np.asarray(w_qkv, dtype=np.float32)),
        "wout": np.ascontiguousarray(np.asarray(w_out, dtype=np.float32)),
        "ln_g": np.ascontiguousarray(np.asarray(ln_g, dtype=np.float32)),
        "ln_b": np.ascontiguousarray(np.asarray(ln_b, dtype=np.float32)),
        "mem_beta": np.ascontiguousarray(np.asarray(mem_beta, dtype=np.float32).reshape(H)),
        "rhs": np.ascontiguousarray(np.asarray(rhs, dtype=np.float32)),
        "cos_hi": c["cos_hi"], "cos_lo": c["cos_lo"],
        "sin_hi": c["sin_hi"], "sin_lo": c["sin_lo"],
        "maska": c["maska"], "ident": c["ident"], "identf": c["identf"],
    }


AF = mybir.ActivationFunctionType
ALU = mybir.AluOpType
NTT, NBL, NF = 2, 2, 8





RHSBUFS = [("XT", 0), ("XT", 1), ("X1", 0), ("X1", 1),
           ("XN2", 0), ("XN2", 1), ("FOUT", 0), ("FOUT", 1)]

PNBANKS = [("pp0", 7, 0), ("pp1", 7, 0), ("pp2", 2, 0)]
SDBANKS = [("pa0", 7, 0), ("pa1", 7, 0), ("pp2", 2, 260)]


def pn_slot(h):
    bank, nh, base = PNBANKS[min(h // 7, 2)]
    return bank, base + 65 * (h - 7 * min(h // 7, 2))


def sd_slot(h):
    bank, nh, base = SDBANKS[min(h // 7, 2)]
    return bank, base + 65 * (h - 7 * min(h // 7, 2))


def a1(ap):
    kk = ap.shape[-1]
    return ap.rearrange("p (a k) -> p a k", a=1)


def bc_inner(ap_2d, n_inner):
    P_, K_ = ap_2d.shape
    return ap_2d.rearrange("p (k a) -> p k a", a=1).broadcast_to([P_, K_, n_inner])




def bcast_row(nc, p, sb, src_row, dst, n, bank_name="pa0"):
    W = sb
    for c0 in range(0, n, 512):
        w = min(512, n - c0)
        t = p.pre("t")
        p.emit("t", t.matmul(W[bank_name][:, 0:w], W["ONESF"][:, :],
                             src_row[:, c0:c0 + w], start=True, stop=True,
                             skip_group_check=True))
        p.fence("t")
        v = p.pre("v")
        p.emit("v", v.tensor_copy(out=dst[:, c0:c0 + w], in_=W[bank_name][:, 0:w]))
        p.fence("v")


def prologue(nc, p, sb, L):
    W = sb
    for i in range(8):
        p.dma(out=W["WQKV"][:, 3072 * i:3072 * (i + 1)],
              in_=L["dwqkv"][128 * i:128 * (i + 1), :], queue="g")
        p.dma(out=W["WOUT"][:, 1024 * i:1024 * (i + 1)],
              in_=L["dwout"][128 * i:128 * (i + 1), :], queue="g")
    p.dma(out=W["GCOL"][:, :], in_=L["dg"].rearrange("(i p) -> p i", p=128), queue="s", allow_slow=True)
    p.dma(out=W["BCOL"][:, :], in_=L["db"].rearrange("(i p) -> p i", p=128), queue="s", allow_slow=True)
    p.dma(out=W["C64"][0:1, 0:16], in_=L["dbeta"].rearrange("(a h) -> a h", a=1), queue="s")
    p.dma(out=W["MASKA"][:, :], in_=L["dmask"], queue="s")
    p.dma(out=W["IDENT"][:, :], in_=L["dident"], queue="s")
    p.dma(out=W["IDENTF"][:, :], in_=L["didentf"], queue="s")
    p.dma(out=W["FTMP"][:, 0:32], in_=L["dch"], queue="s")
    p.dma(out=W["FTMP"][:, 32:64], in_=L["dcl"], queue="s")
    p.dma(out=W["FTMP"][:, 64:96], in_=L["dsh"], queue="s")
    p.dma(out=W["FTMP"][:, 96:128], in_=L["dsl"], queue="s")
    for i, (bn, half) in enumerate(RHSBUFS):
        p.dma(out=W[bn][:, 1024 * half:1024 * (half + 1)],
              in_=L["drhs"][128 * i:128 * (i + 1), :], queue="s")
    p.barrier_dma()

    v = p.pre("v")
    p.emit("v", v.tensor_tensor(out=W["COS2"][:, :], in0=W["FTMP"][:, 0:32],
                                in1=W["FTMP"][:, 32:64], op=ALU.add))
    p.emit("v", v.tensor_tensor(out=W["SIN2"][:, :], in0=W["FTMP"][:, 64:96],
                                in1=W["FTMP"][:, 96:128], op=ALU.add))
    p.emit("v", v.tensor_copy(out=W["C64B"][:, 0:8], in_=W["BCOL"][:, :]))
    p.emit("v", v.memset(W["ONESF"][:, :], 1.0))
    p.fence("v")
    a = p.pre("a")
    p.emit("a", a.activation(out=W["C64"][0:1, 16:32], in_=W["C64"][0:1, 0:16],
                             func=AF.Sigmoid))
    p.fence("a")
    bcast_row(nc, p, sb, W["C64"][0:1, 16:32], W["BETA"][:, :], 16)
    v = p.pre("v")
    p.emit("v", v.tensor_scalar(out=W["BETB"][:, :], in0=W["BETA"][:, :], scalar1=-1.0,
                                scalar2=1.0, op0=ALU.mult, op1=ALU.add))
    p.fence("v")

    t = p.pre("t")
    for j in range(16):
        for i in range(8):
            p.emit("t", t.matmul(W["pa0"][:, j:j + 1],
                                 W["WQKV"][:, 3072 * i + 128 * j:3072 * i + 128 * (j + 1)],
                                 W["C64B"][:, i:i + 1],
                                 start=(i == 0 and j == 0), stop=(i == 7),
                                 skip_group_check=True))
    for nch in range(2):
        bank = W["pa1"] if nch == 0 else W["pb0"]
        for i in range(8):
            c0 = 3072 * i + 2048 + 512 * nch
            p.emit("t", t.matmul(bank[0:1, 0:512],
                                 W["C64B"][:, i:i + 1], W["WQKV"][:, c0:c0 + 512],
                                 start=(i == 0), stop=(i == 7),
                                 skip_group_check=True))
    p.fence("t")
    v = p.pre("v")
    p.emit("v", v.tensor_copy(out=W["BIASF"][:, :], in_=W["pa0"][:, 0:16]))
    p.emit("v", v.tensor_copy(out=W["BVROWF"][:, 0:512], in_=W["pa1"][0:1, 0:512]))
    p.emit("v", v.tensor_copy(out=W["BVROWF"][:, 512:1024], in_=W["pb0"][0:1, 0:512]))
    for i in range(8):
        p.emit("v", v.tensor_scalar(out=W["WQKV"][:, 3072 * i:3072 * (i + 1)],
                                    in0=W["WQKV"][:, 3072 * i:3072 * (i + 1)],
                                    scalar1=W["GCOL"][:, i:i + 1], scalar2=None,
                                    op0=ALU.mult, op1=ALU.bypass))
    p.fence("v")
    bcast_row(nc, p, sb, W["BVROWF"][0:1, :], W["VBIASB"][:, :], 1024, bank_name="pa1")
    v = p.pre("v")
    p.emit("v", v.memset(W["ONESH"][:, :], 0.0078125))
    p.emit("v", v.tensor_copy(out=W["VBIASH"][:, :], in_=W["VBIASB"][:, :]))
    p.fence("v")

    for i in range(8):
        bn, half = RHSBUFS[i]
        t = p.pre("t")
        for j in range(8):
            bank = W["pb0"] if j < 4 else W["pb1"]
            p.emit("t", t.transpose(bank[:, 128 * (j % 4):128 * (j % 4 + 1)],
                                    sb[bn][:, 1024 * half + 128 * j:1024 * half + 128 * (j + 1)],
                                    W["IDENTF"][:, :]))
        p.fence("t")
        v = p.pre("v")
        for j in range(8):
            bank = W["pb0"] if j < 4 else W["pb1"]
            p.emit("v", v.tensor_reduce(out=W["C64"][:, 48 + j:49 + j],
                                        in_=bank[:, 128 * (j % 4):128 * (j % 4 + 1)],
                                        axis=mybir.AxisListType.X, op=ALU.max,
                                        apply_absolute_value=True))
        p.mini("v")
        if i == 0:
            p.emit("v", v.tensor_copy(out=W["C64"][:, 32:40], in_=W["C64"][:, 48:56]))
        else:
            p.emit("v", v.tensor_tensor(out=W["C64"][:, 32:40], in0=W["C64"][:, 32:40],
                                        in1=W["C64"][:, 48:56], op=ALU.max))
        p.mini("v")
        p.fence("v")
    v = p.pre("v")
    p.emit("v", v.tensor_scalar(out=W["C64"][:, 32:40], in0=W["C64"][:, 32:40],
                                scalar1=1e-12, scalar2=None, op0=ALU.max, op1=ALU.bypass))
    p.fence("v")
    t = p.pre("t")
    p.emit("t", t.transpose(W["pb0"][0:8, 0:128], W["C64"][:, 32:40], W["IDENTF"][:, :]))
    p.fence("t")
    v = p.pre("v")
    p.emit("v", v.tensor_copy(out=W["ROWF"][0:8, 0:128], in_=W["pb0"][0:8, 0:128]))
    p.fence("v")
    for j in range(8):
        p.dma(out=W["RSB"][0:1, 128 * j:128 * (j + 1)],
              in_=W["ROWF"][j:j + 1, 0:128], queue="s")
    p.barrier_dma()
    v = p.pre("v")
    p.emit("v", v.reciprocal(out=W["INVSB"][0:1, 0:1024], in_=W["RSB"][0:1, 0:1024]))
    p.mini("v")
    p.emit("v", v.tensor_scalar(out=W["INVSB"][0:1, 0:1024], in0=W["INVSB"][0:1, 0:1024],
                                scalar1=127.0, scalar2=None, op0=ALU.mult, op1=ALU.bypass))
    p.emit("v", v.memset(W["C64"][:, 49:50], 127.0))
    p.mini("v")
    p.emit("v", v.reciprocal(out=W["C64"][:, 50:51], in_=W["C64"][:, 49:50]))
    p.mini("v")
    p.emit("v", v.tensor_scalar(out=W["RSB"][0:1, 0:1024], in0=W["RSB"][0:1, 0:1024],
                                scalar1=W["C64"][0:1, 50:51], scalar2=None,
                                op0=ALU.mult, op1=ALU.bypass))
    p.fence("v")
    bcast_row(nc, p, sb, W["INVSB"][0:1, 0:1024], W["INVSB"][:, :], 1024, bank_name="pa0")
    bcast_row(nc, p, sb, W["RSB"][0:1, 0:1024], W["RSB"][:, :], 1024, bank_name="pa1")
    for i in range(8):
        bn, half = RHSBUFS[i]
        src = sb[bn][:, 1024 * half:1024 * (half + 1)]
        quantize_tile(nc, p, sb, src_f32=src, invs=W["INVSB"][:, :],
                      dst=W["QRHS"][:, 1024 * i:1024 * (i + 1)], per_partition=False)

    v = p.pre("v")
    p.emit("v", v.memset(W["STF"][:, :], 0.0))
    p.emit("v", v.memset(W["STBP"][:, :], 0.0))
    p.emit("v", v.memset(W["UB"][:, :], 0.0))
    p.mini("v")
    stf3 = W["STF"].ap().rearrange("p (pr c) -> p pr c", c=65)
    p.emit("v", v.memset(stf3[:, :, 64:65], 1e-6))
    p.fence("v")
    state_cast(nc, p, sb)
    v = p.pre("v")
    vb4 = W["VB"].ap().rearrange("p (t2 h c) -> p t2 h c", t2=2, c=65)
    p.emit("v", v.memset(vb4[:, :, :, 64:65], 1.0))
    p.fence("v")


def state_cast(nc, p, sb):
    W = sb
    stf3 = W["STF"].ap().rearrange("p (pr c) -> p pr c", c=65)
    stbp2 = W["STBP"].ap().rearrange("p (m r) -> p m r", r=130)
    v = p.pre("v")
    p.emit("v", v.tensor_copy(out=stbp2[0:64, :, 0:65], in_=stf3[0:64, :, :]))
    p.emit("v", v.tensor_copy(out=stbp2[64:128, :, 65:130], in_=stf3[64:128, :, :]))
    p.fence("v")


def quantize_tile(nc, p, sb, src_f32, invs, dst, per_partition, invs_col=None):
    W = sb
    v = p.pre("v")
    p.mini("v")
    if per_partition:
        p.emit("v", v.tensor_scalar(out=W["FTMP"][:, :], in0=src_f32, scalar1=invs_col,
                                    scalar2=None, op0=ALU.mult, op1=ALU.bypass))
    else:
        p.emit("v", v.tensor_tensor(out=W["FTMP"][:, :], in0=src_f32, in1=invs,
                                    op=ALU.mult))
    p.fence("v")
    a = p.pre("a")
    p.emit("a", a.activation(out=W["QI"][:, 0:1024], in_=W["FTMP"][:, :], func=AF.Copy))
    p.fence("a")
    v = p.pre("v")
    p.emit("v", v.scalar_tensor_tensor(out=W["TQ"][:, 0:1024], in0=W["FTMP"][:, :],
                                       scalar=1.0, in1=W["QI"][:, 0:1024],
                                       op0=ALU.mult, op1=ALU.subtract))
    p.mini("v")
    p.emit("v", v.scalar_tensor_tensor(out=W["TMPA"][:, :], in0=W["TQ"][:, 0:1024],
                                       scalar=0.5, in1=W["QI"][:, 0:1024],
                                       op0=ALU.is_gt, op1=ALU.add))
    p.mini("v")
    p.emit("v", v.scalar_tensor_tensor(out=W["TMPB"][:, :], in0=W["TQ"][:, 0:1024],
                                       scalar=-0.5, in1=W["TMPA"][:, :],
                                       op0=ALU.is_ge, op1=ALU.add))
    p.mini("v")
    p.emit("v", v.tensor_scalar(out=W["TMPA"][:, :], in0=W["TMPB"][:, :],
                                scalar1=-126.0, scalar2=None, op0=ALU.max, op1=ALU.bypass))
    p.mini("v")
    p.emit("v", v.tensor_scalar(out=dst, in0=W["TMPA"][:, :], scalar1=128.0,
                                scalar2=-1.0, op0=ALU.min, op1=ALU.add))
    p.fence("v")


def ln_fused(nc, p, sb, src_ap, dst_ap, cbase, double):
    W = sb
    C = W["C64"]
    v = p.pre("v")
    p.emit("v", v.bn_stats(out=C[:, cbase:cbase + 6], in_=src_ap[:, 0:512]))
    p.emit("v", v.bn_stats(out=C[:, cbase + 6:cbase + 12], in_=src_ap[:, 512:1024]))
    p.mini("v")
    p.emit("v", v.bn_aggr(out=C[:, cbase + 12:cbase + 14], in_=C[:, cbase:cbase + 12]))
    p.mini("v")
    m1 = C[:, cbase + 12:cbase + 13]
    v1 = C[:, cbase + 13:cbase + 14]
    A = C[:, cbase:cbase + 1]
    if double:
        p.emit("v", v.tensor_scalar(out=A, in0=v1, scalar1=1.00001,
                                    scalar2=1.0000100000000001e-05,
                                    op0=ALU.mult, op1=ALU.add))
    else:
        p.emit("v", v.tensor_scalar(out=A, in0=v1, scalar1=1e-5, scalar2=None,
                                    op0=ALU.add, op1=ALU.bypass))
    p.fence("v", waiters=("a", "v"))
    a = p.pre("a")
    p.emit("a", a.activation(out=C[:, cbase + 5:cbase + 6], in_=A, func=AF.Sqrt))
    p.fence("a", waiters=("v", "a"))
    v = p.pre("v")
    p.emit("v", v.reciprocal(out=C[:, cbase + 7:cbase + 8], in_=C[:, cbase + 5:cbase + 6]))
    p.mini("v")
    R = C[:, cbase + 7:cbase + 8]
    p.emit("v", v.tensor_scalar(out=C[:, cbase + 9:cbase + 10], in0=m1, scalar1=R,
                                scalar2=-1.0, op0=ALU.mult, op1=ALU.mult))
    p.fence("v")
    if dst_ap is not None:
        a = p.pre("a")
        p.emit("a", a.activation(out=dst_ap, in_=src_ap, func=AF.Identity,
                                 bias=C[:, cbase + 9:cbase + 10], scale=R))
        p.fence("a")
    return R, C[:, cbase + 9:cbase + 10]


def chunk_stages(nc, p, sb, L, c):
    W = sb
    p.dma(out=W["XT"].ap().rearrange("p (a d) -> p a d", a=2),
          in_=L["dx"][CHUNK * c:CHUNK * (c + 1), :].rearrange("(a p) d -> p a d", p=128),
          queue="s")
    p.barrier_dma()

    for tt in range(NTT):
        ln_fused(nc, p, sb, W["XT"][:, 1024 * tt:1024 * (tt + 1)],
                 W["XN"][:, 1024 * tt:1024 * (tt + 1)], cbase=16 * tt, double=True)

    for tt in range(NTT):
        t = p.pre("t")
        for f in range(8):
            p.emit("t", t.transpose(W["ptr"][:, 128 * f:128 * (f + 1)],
                                    W["XN"][:, 1024 * tt + 128 * f:1024 * tt + 128 * (f + 1)],
                                    W["IDENT"][:, :]))
        p.fence("t")
        a = p.pre("a")
        p.emit("a", a.activation(
            out=W["XNT"].ap().rearrange("p (f b) -> p f b", b=256)[:, :, 128 * tt:128 * (tt + 1)],
            in_=W["ptr"][:, :], func=AF.Copy))
        p.fence("a")

    for j in range(16):
        bank = W["pa0"] if j % 2 == 0 else W["pa1"]
        col = 256 * ((j // 2) % 2)
        ps = bank[:, col:col + 256]
        t = p.pre("t")
        for i in range(8):
            p.emit("t", t.matmul(ps, W["WQKV"][:, 3072 * i + 128 * j:3072 * i + 128 * (j + 1)],
                                 W["XNT"][:, 256 * i:256 * (i + 1)],
                                 start=(i == 0), stop=(i == 7), skip_group_check=True))
        p.fence("t")
        a = p.pre("a")
        dstbuf = W["QN"] if j < 8 else W["KN"]
        pf = j % 8
        p.emit("a", a.activation(out=dstbuf[:, 256 * pf:256 * (pf + 1)], in_=ps,
                                 func=AF.Identity, bias=W["BIASF"][:, j:j + 1],
                                 scale=np.float32(1.0)))
        p.fence("a")

    vb4 = W["VB"].ap().rearrange("p (t2 h cc) -> p t2 h cc", t2=2, cc=65)
    vcnt = {}
    for gidx, (tt, nch) in enumerate([(a_, b_) for a_ in range(NTT) for b_ in range(2)]):
        ps = (W["pb0"] if nch == 0 else W["pb1"])[:, 0:512]
        if gidx >= 2:
            p.wait_cnt("t", "a", vcnt[gidx - 2])
        t = p.pre("t")
        for i in range(8):
            c0 = 3072 * i + 2048 + 512 * nch
            p.emit("t", t.matmul(ps, W["XNT"][:, 256 * i + 128 * tt:256 * i + 128 * (tt + 1)],
                                 W["WQKV"][:, c0:c0 + 512],
                                 start=(i == 0), stop=False, skip_group_check=True))
        p.emit("t", t.matmul(ps, W["ONESH"][:, :],
                             W["VBIASH"][:, 512 * nch:512 * (nch + 1)],
                             start=False, stop=True, skip_group_check=True))
        tcnt = p.fence("t", waiters=("a", "t"))
        p.wait_cnt("a", "t", tcnt)
        a = p.pre("a")
        p.emit("a", a.activation(out=vb4[:, tt, 8 * nch:8 * (nch + 1), 0:64],
                                 in_=ps.rearrange("p (h e) -> p h e", e=64), func=AF.Copy))
        vcnt[gidx] = p.fence("a", waiters=("a", "v", "g", "s"))

    cosb4 = a1(W["COS2"][:, :]).broadcast_to([128, 32, 32])
    sinb4 = a1(W["SIN2"][:, :]).broadcast_to([128, 32, 32])
    for srcb, dstb, eng, tm, rb in (("QN", "QR", "v", "TMPA", "TBF"),
                                    ("KN", "KR", "v", "TMPB", "UB")):
        for hf in range(2):
            cols = slice(1024 * hf, 1024 * (hf + 1))
            src_ = W[srcb][:, cols]
            e = p.pre(eng)
            p.emit(eng, e.tensor_copy(out=W[rb][0:32, 0:1024], in_=src_[32:64, :]))
            p.emit(eng, e.tensor_copy(out=W[rb][32:64, 0:1024], in_=src_[0:32, :]))
            p.emit(eng, e.tensor_copy(out=W[rb][64:96, 0:1024], in_=src_[96:128, :]))
            p.emit(eng, e.tensor_copy(out=W[rb][96:128, 0:1024], in_=src_[64:96, :]))
            p.mini(eng)
            p.emit(eng, e.tensor_tensor(out=W[tm][:, :].rearrange("p (r k) -> p r k", k=32),
                                        in0=src_.rearrange("p (r k) -> p r k", k=32),
                                        in1=cosb4, op=ALU.mult))
            p.mini(eng)
            p.emit(eng, e.tensor_tensor(out=W[rb][:, 0:1024].rearrange("p (r k) -> p r k", k=32),
                                        in0=W[rb][:, 0:1024].rearrange("p (r k) -> p r k", k=32),
                                        in1=sinb4, op=ALU.mult))
            p.mini(eng)
            p.emit(eng, e.tensor_tensor(out=W[dstb][:, cols], in0=W[tm][:, :],
                                        in1=W[rb][:, 0:1024], op=ALU.add))
            p.fence(eng)

    for srcb, dstb in (("QN", "SQ"), ("KN", "SK")):
        for hf in range(2):
            cols = slice(1024 * hf, 1024 * (hf + 1))
            src = W[srcb][:, cols]
            a = p.pre("a")
            p.emit("a", a.activation(out=W["TMPA"][:, :], in_=src, func=AF.Relu,
                                     scale=np.float32(-1.0)))
            p.fence("a", waiters=("a",))
            p.emit("a", a.activation(out=W["TMPB"][:, :], in_=W["TMPA"][:, :], func=AF.Exp,
                                     scale=np.float32(-1.0)))
            p.fence("a", waiters=("a", "v"))
            v = p.pre("v")
            p.emit("v", v.scalar_tensor_tensor(out=W[dstb][:, cols],
                                               in0=src, scalar=0.0, in1=W["TMPB"][:, :],
                                               op0=ALU.max, op1=ALU.add))
            p.fence("v")

    for tt in range(NTT):
        t = p.pre("t")
        for pf in range(8):
            p.emit("t", t.transpose(W["ptr"][:, 128 * pf:128 * (pf + 1)],
                                    W["SK"][:, 256 * pf + 128 * tt:256 * pf + 128 * (tt + 1)],
                                    W["IDENT"][:, :]))
        p.fence("t")
        a = p.pre("a")
        p.emit("a", a.activation(out=W["SKT"][:, 1024 * tt:1024 * (tt + 1)],
                                 in_=W["ptr"][:, :], func=AF.Copy))
        p.fence("a")

    for bl in range(NBL):
        attn_block(nc, p, sb, L, c, bl)

    for tt in range(NTT):
        t = p.pre("t")
        for f in range(8):
            p.emit("t", t.transpose(W["ptr"][:, 128 * f:128 * (f + 1)],
                                    W["OB"][:, 1024 * tt + 128 * f:1024 * tt + 128 * (f + 1)],
                                    W["IDENT"][:, :]))
        p.fence("t")
        a = p.pre("a")
        p.emit("a", a.activation(
            out=W["OT"].ap().rearrange("p (f b) -> p f b", b=256)[:, :, 128 * tt:128 * (tt + 1)],
            in_=W["ptr"][:, :], func=AF.Copy))
        p.fence("a")
    for tt in range(NTT):
        for nch in range(2):
            ps = (W["pb0"] if nch == 0 else W["pb1"])[:, 0:512]
            t = p.pre("t")
            for i in range(8):
                p.emit("t", t.matmul(ps, W["OT"][:, 256 * i + 128 * tt:256 * i + 128 * (tt + 1)],
                                     W["WOUT"][:, 1024 * i + 512 * nch:1024 * i + 512 * (nch + 1)],
                                     start=(i == 0), stop=(i == 7), skip_group_check=True))
            p.fence("t")
            a = p.pre("a")
            p.emit("a", a.activation(out=W["FTMP"][:, 512 * nch:512 * (nch + 1)], in_=ps,
                                     func=AF.Copy))
            p.fence("a")
        R, NMR = ln_fused(nc, p, sb, W["FTMP"][:, 0:1024], None, cbase=32, double=False)
        a = p.pre("a")
        p.emit("a", a.activation(out=W["TMPA"][:, :], in_=W["FTMP"][:, 0:1024],
                                 func=AF.Identity, bias=NMR, scale=R))
        p.fence("a")
        v = p.pre("v")
        p.emit("v", v.tensor_tensor(out=W["X1"][:, 1024 * tt:1024 * (tt + 1)],
                                    in0=W["XT"][:, 1024 * tt:1024 * (tt + 1)],
                                    in1=W["TMPA"][:, :], op=ALU.add))
        p.fence("v")

    for tt in range(NTT):
        ln_fused(nc, p, sb, W["X1"][:, 1024 * tt:1024 * (tt + 1)],
                 W["XN2"][:, 1024 * tt:1024 * (tt + 1)], cbase=32, double=False)
        v = p.pre("v")
        p.emit("v", v.tensor_reduce(out=W["C64"][:, 44:45], in_=W["XN2"][:, 1024 * tt:1024 * (tt + 1)],
                                    axis=mybir.AxisListType.X, op=ALU.max,
                                    apply_absolute_value=True))
        p.mini("v")
        p.emit("v", v.tensor_scalar(out=W["C64"][:, 45:46], in0=W["C64"][:, 44:45],
                                    scalar1=1e-12, scalar2=None, op0=ALU.max, op1=ALU.bypass))
        p.mini("v")
        p.emit("v", v.reciprocal(out=W["C64"][:, 46:47], in_=W["C64"][:, 45:46]))
        p.mini("v")
        p.emit("v", v.tensor_scalar(out=W["C64"][:, 46:47], in0=W["C64"][:, 46:47],
                                    scalar1=127.0, scalar2=None, op0=ALU.mult, op1=ALU.bypass))
        p.emit("v", v.tensor_scalar(out=W["C64"][:, 47:48], in0=W["C64"][:, 45:46],
                                    scalar1=W["C64"][:, 50:51], scalar2=None,
                                    op0=ALU.mult, op1=ALU.bypass))
        p.fence("v")
        quantize_tile(nc, p, sb, src_f32=W["XN2"][:, 1024 * tt:1024 * (tt + 1)], invs=None,
                      dst=W["QLTOK"][:, 1024 * tt:1024 * (tt + 1)], per_partition=True,
                      invs_col=W["C64"][:, 46:47])
        t = p.pre("t")
        for f in range(8):
            p.emit("t", t.transpose(W["ptr"][:, 128 * f:128 * (f + 1)],
                                    W["QLTOK"][:, 1024 * tt + 128 * f:1024 * tt + 128 * (f + 1)],
                                    W["IDENT"][:, :]))
        p.fence("t")
        a = p.pre("a")
        p.emit("a", a.activation(
            out=W["QLT"].ap().rearrange("p (f b) -> p f b", b=256)[:, :, 128 * tt:128 * (tt + 1)],
            in_=W["ptr"][:, :], func=AF.Copy))
        p.fence("a")
        for nch in range(2):
            ps = (W["pb0"] if nch == 0 else W["pb1"])[:, 0:512]
            t = p.pre("t")
            for i in range(8):
                p.emit("t", t.matmul(ps, W["QLT"][:, 256 * i + 128 * tt:256 * i + 128 * (tt + 1)],
                                     W["QRHS"][:, 1024 * i + 512 * nch:1024 * i + 512 * (nch + 1)],
                                     start=(i == 0), stop=(i == 7), skip_group_check=True))
            p.fence("t")
            v = p.pre("v")
            p.emit("v", v.tensor_scalar(out=W["FTMP"][:, 512 * nch:512 * (nch + 1)], in0=ps,
                                        scalar1=W["C64"][:, 47:48], scalar2=None,
                                        op0=ALU.mult, op1=ALU.bypass))
            p.mini("v")
            p.emit("v", v.tensor_tensor(out=W["FTMP"][:, 512 * nch:512 * (nch + 1)],
                                        in0=W["FTMP"][:, 512 * nch:512 * (nch + 1)],
                                        in1=W["RSB"][:, 512 * nch:512 * (nch + 1)], op=ALU.mult))
            p.fence("v")
        R, NMR = ln_fused(nc, p, sb, W["FTMP"][:, 0:1024], None, cbase=32, double=False)
        a = p.pre("a")
        p.emit("a", a.activation(out=W["TMPA"][:, :], in_=W["FTMP"][:, 0:1024],
                                 func=AF.Identity, bias=NMR, scale=R))
        p.fence("a")
        v = p.pre("v")
        p.emit("v", v.tensor_tensor(out=W["FOUT"][:, 1024 * tt:1024 * (tt + 1)],
                                    in0=W["X1"][:, 1024 * tt:1024 * (tt + 1)],
                                    in1=W["TMPA"][:, :], op=ALU.add))
        p.fence("v")
    p.dma(out=L["dout"][CHUNK * c:CHUNK * (c + 1), :].rearrange("(a p) d -> p a d", p=128),
          in_=W["FOUT"].ap().rearrange("p (a d) -> p a d", a=2), queue="s")

    if c == 0 and L["debug_dump"]:
        p.barrier_dma()
        for name, buf, shape, dt in L["debug_dump"]:
            if name in ("PSA", "PSB"):
                continue
            p.dma(out=L["ddbg"][name], in_=W[buf][:, :], queue="s")
        p.barrier_dma()


def attn_block(nc, p, sb, L, c, bl):
    W = sb
    maskb = a1(W["MASKA"][:, :]).broadcast_to([128, 4, 128])
    eb4 = W["EB"].ap().rearrange("p (w j two q) -> p w j two q", w=2, j=4, two=2)
    for wv in range(2):
        t = p.pre("t")
        for h in range(8 * wv, 8 * wv + 8):
            pf, half = h // 2, h % 2
            rows = slice(64 * half, 64 * (half + 1))
            bank = W["pa0"] if half == 0 else W["pa1"]
            slot = (h // 2) % 4
            tok = slice(256 * pf + 128 * bl, 256 * pf + 128 * (bl + 1))
            p.emit("t", t.matmul(bank[:, 128 * slot:128 * (slot + 1)],
                                 W["KR"][rows, tok], W["QR"][rows, tok],
                                 start=True, stop=True, skip_group_check=True))
        p.fence("t")
        v = p.pre("v")
        p.emit("v", v.tensor_tensor(out=W["FTMP"][:, 0:512].rearrange("p (s q) -> p s q", q=128),
                                    in0=W["pa0"].ap().rearrange("p (s q) -> p s q", q=128),
                                    in1=maskb, op=ALU.add))
        p.emit("v", v.tensor_tensor(out=W["FTMP"][:, 512:1024].rearrange("p (s q) -> p s q", q=128),
                                    in0=W["pa1"].ap().rearrange("p (s q) -> p s q", q=128),
                                    in1=maskb, op=ALU.add))
        p.fence("v")
        a = p.pre("a")
        p.emit("a", a.activation(out=eb4[:, wv, :, 0, :],
                                 in_=W["FTMP"][:, 0:512], func=AF.Exp, scale=np.float32(0.125)))
        p.emit("a", a.activation(out=eb4[:, wv, :, 1, :],
                                 in_=W["FTMP"][:, 512:1024], func=AF.Exp, scale=np.float32(0.125)))
        p.fence("a")

    ghb4 = W["GHB"].ap().rearrange("p (w j two q) -> p w j two q", w=4, j=2, two=2)
    for wv in range(4):
        t = p.pre("t")
        for h in range(4 * wv, 4 * wv + 4):
            pf, half = h // 2, h % 2
            rows = slice(64 * half, 64 * (half + 1))
            bank = W["pb0"] if half == 0 else W["pb1"]
            slot = (h // 2) % 2
            tok = slice(256 * pf + 128 * bl, 256 * pf + 128 * (bl + 1))
            p.emit("t", t.matmul(bank[:, 256 * slot:256 * slot + 128],
                                 W["SK"][rows, tok], W["SQ"][rows, tok],
                                 start=True, stop=True, skip_group_check=True))
            p.emit("t", t.matmul(bank[:, 256 * slot + 128:256 * slot + 256],
                                 W["SK"][rows, tok], W["SK"][rows, tok],
                                 start=True, stop=True, skip_group_check=True))
        p.fence("t")
        a = p.pre("a")
        p.emit("a", a.activation(out=ghb4[:, wv, :, 0, :], in_=W["pb0"][:, :], func=AF.Copy))
        p.emit("a", a.activation(out=ghb4[:, wv, :, 1, :], in_=W["pb1"][:, :], func=AF.Copy))
        p.fence("a")

    t = p.pre("t")
    for h in range(H):
        pf = h // 2
        tok = slice(256 * pf + 128 * bl, 256 * pf + 128 * (bl + 1))
        bank, colh = pn_slot(h)
        p.emit("t", t.matmul(W[bank][:, colh:colh + 65],
                             W["SK"][:, tok], W["STBP"][:, 65 * h:65 * (h + 1)],
                             start=(h in (0, 7, 14)), stop=True, skip_group_check=True))
    p.fence("t")

    t = p.pre("t")
    for h in range(H):
        pf = h // 2
        tok = slice(256 * pf + 128 * bl, 256 * pf + 128 * (bl + 1))
        bank, colh = sd_slot(h)
        p.emit("t", t.matmul(W[bank][:, colh:colh + 65],
                             W["SQ"][:, tok], W["STBP"][:, 65 * h:65 * (h + 1)],
                             start=(h in (0, 7)), stop=True, skip_group_check=True))
    p.fence("t")

    v = p.pre("v")
    p.emit("v", v.memset(W["UB"][:, :], 0.0))
    tbf3z = W["TBF"].ap().rearrange("p (h cc) -> p h cc", cc=65)
    p.emit("v", v.memset(tbf3z[:, :, 64:65], 0.0))
    p.fence("v")
    vb4 = W["VB"].ap().rearrange("p (t2 h cc) -> p t2 h cc", t2=2, cc=65)
    ub3 = W["UB"].ap().rearrange("p (h cc) -> p h cc", cc=65)
    tb3f = W["TBF"].ap().rearrange("p (h cc) -> p h cc", cc=65)
    for s in range(4):
        rows = slice(32 * s, 32 * (s + 1))
        if s > 0:
            t = p.pre("t")
            for h in range(H):
                bank, colh = pn_slot(h)
                p.emit("t", t.matmul(W[bank][rows, colh:colh + 65],
                                     W["GHB"][:, 256 * h + 128 + 32 * s:256 * h + 128 + 32 * (s + 1)],
                                     W["UB"][:, 65 * h:65 * (h + 1)],
                                     start=False, stop=True, tile_position=(0, 32 * s),
                                     skip_group_check=True))
            for h in range(H):
                bank, colh = sd_slot(h)
                p.emit("t", t.matmul(W[bank][rows, colh:colh + 65],
                                     W["GHB"][:, 256 * h + 32 * s:256 * h + 32 * (s + 1)],
                                     W["UB"][:, 65 * h:65 * (h + 1)],
                                     start=False, stop=True, tile_position=(0, 32 * s),
                                     skip_group_check=True))
            p.fence("t")
        v = p.pre("v")
        for bi, (bank, nh, base) in enumerate(PNBANKS):
            dens = W[bank][:, base:base + 65 * nh].rearrange("p (hh e) -> p hh e", e=65)[rows, 0:nh, 64:65]
            p.emit("v", v.reciprocal(out=bc_inner(W["RUF"][rows, 7 * bi:7 * bi + nh], 1),
                                     in_=dens))
        p.mini("v")
        for bi, (bank, nh, base) in enumerate(PNBANKS):
            nums = W[bank][:, base:base + 65 * nh].rearrange("p (hh e) -> p hh e", e=65)[rows, 0:nh, 0:64]
            p.emit("v", v.tensor_tensor(out=tb3f[rows, 7 * bi:7 * bi + nh, 0:64], in0=nums,
                                        in1=bc_inner(W["RUF"][rows, 7 * bi:7 * bi + nh], 64),
                                        op=ALU.mult))
        p.mini("v")
        p.emit("v", v.tensor_tensor(out=ub3[rows, :, 0:65], in0=vb4[rows, bl, :, 0:65],
                                    in1=tb3f[rows, :, 0:65], op=ALU.subtract))
        p.fence("v")

    t = p.pre("t")
    for h in range(H):
        bank, colh = pn_slot(h)
        p.emit("t", t.matmul(W[bank][:, colh:colh + 65],
                             W["EB"][:, 128 * h:128 * (h + 1)],
                             W["VB"][:, 1040 * bl + 65 * h:1040 * bl + 65 * (h + 1)],
                             start=(h in (0, 7, 14)), stop=True, skip_group_check=True))
    p.fence("t")

    v = p.pre("v")
    for bi, (bank, nh, base) in enumerate(SDBANKS):
        dens = W[bank][:, base:base + 65 * nh].rearrange("p (hh e) -> p hh e", e=65)[:, 0:nh, 64:65]
        p.emit("v", v.tensor_scalar(out=bc_inner(W["RQF"][:, 7 * bi:7 * bi + nh], 1),
                                    in0=dens, scalar1=1e-6, scalar2=None,
                                    op0=ALU.add, op1=ALU.bypass))
    p.mini("v")
    p.emit("v", v.reciprocal(out=W["RUF"][:, 0:16], in_=W["RQF"][:, 0:16]))
    p.mini("v")
    p.emit("v", v.tensor_tensor(out=W["RQF"][:, 0:16], in0=W["RUF"][:, 0:16],
                                in1=W["BETA"][:, :], op=ALU.mult))
    p.mini("v")
    for bi, (bank, nh, base) in enumerate(PNBANKS):
        dens = W[bank][:, base:base + 65 * nh].rearrange("p (hh e) -> p hh e", e=65)[:, 0:nh, 64:65]
        p.emit("v", v.reciprocal(out=bc_inner(W["RUF"][:, 7 * bi:7 * bi + nh], 1), in_=dens))
    p.mini("v")
    p.emit("v", v.tensor_tensor(out=W["RUF"][:, 0:16], in0=W["RUF"][:, 0:16],
                                in1=W["BETB"][:, :], op=ALU.mult))
    p.mini("v")
    tm3a = W["TMPA"].ap().rearrange("p (hh e) -> p hh e", e=64)
    tm3b = W["TMPB"].ap().rearrange("p (hh e) -> p hh e", e=64)
    for bi, (bank, nh, base) in enumerate(SDBANKS):
        nums = W[bank][:, base:base + 65 * nh].rearrange("p (hh e) -> p hh e", e=65)[:, 0:nh, 0:64]
        p.emit("v", v.tensor_tensor(out=tm3a[:, 7 * bi:7 * bi + nh, :], in0=nums,
                                    in1=bc_inner(W["RQF"][:, 7 * bi:7 * bi + nh], 64),
                                    op=ALU.mult))
    for bi, (bank, nh, base) in enumerate(PNBANKS):
        nums = W[bank][:, base:base + 65 * nh].rearrange("p (hh e) -> p hh e", e=65)[:, 0:nh, 0:64]
        p.emit("v", v.tensor_tensor(out=tm3b[:, 7 * bi:7 * bi + nh, :], in0=nums,
                                    in1=bc_inner(W["RUF"][:, 7 * bi:7 * bi + nh], 64),
                                    op=ALU.mult))
    p.mini("v")
    p.emit("v", v.tensor_tensor(out=W["OB"][:, 1024 * bl:1024 * (bl + 1)],
                                in0=W["TMPA"][:, :], in1=W["TMPB"][:, :], op=ALU.add))
    p.fence("v")

    if c == 0 and bl == 1 and L.get("dump_comb"):
        p.dma(out=L["ddbg"]["TMA"], in_=W["TMPA"][:, :], queue="s")
        p.dma(out=L["ddbg"]["TMB"], in_=W["TMPB"][:, :], queue="s")
        p.dma(out=L["ddbg"]["RQBd"], in_=W["RQB"][:, :], queue="s")
        p.dma(out=L["ddbg"]["RSCd"], in_=W["RSBC"][:, :], queue="s")
        p.barrier_dma()
    if c == 0 and bl == 1 and L.get("dump_psum"):
        v = p.pre("v")
        for di, (bk, nh, base) in enumerate(SDBANKS):
            p.emit("v", v.tensor_copy(out=W["XN2"][:, 455 * di:455 * di + 65 * nh],
                                      in_=W[bk][:, base:base + 65 * nh]))
        for di, (bk, nh, base) in enumerate(PNBANKS):
            p.emit("v", v.tensor_copy(out=W["FOUT"][:, 455 * di:455 * di + 65 * nh],
                                      in_=W[bk][:, base:base + 65 * nh]))
        p.emit("v", v.tensor_copy(out=W["FOUT"][:, 1536:1552], in_=W["RQF"][:, :]))
        p.emit("v", v.tensor_copy(out=W["FOUT"][:, 1552:1568], in_=W["RUF"][:, :]))
        p.fence("v")
        p.dma(out=L["ddbg"]["PSA"], in_=W["XN2"][:, :], queue="s")
        p.dma(out=L["ddbg"]["PSB"], in_=W["FOUT"][:, :], queue="s")
        p.barrier_dma()

    t = p.pre("t")
    for h in range(H):
        pr, half = h // 2, h % 2
        bank = W["pb1"] if pr < 4 else W["pb0"]
        colp = 128 * (pr % 4)
        p.emit("t", t.matmul(bank[64 * half:64 * (half + 1), colp:colp + 65],
                             W["SKT"][:, 1024 * bl + 128 * pr + 64 * half:1024 * bl + 128 * pr + 64 * (half + 1)],
                             W["UB"][:, 65 * h:65 * (h + 1)],
                             start=True, stop=True, tile_position=(0, 64 * half),
                             skip_group_check=True))
    p.fence("t")
    v = p.pre("v")
    for bi, bank in enumerate((W["pb1"], W["pb0"])):
        src = bank.ap().rearrange("p (pr e) -> p pr e", e=128)[:, 0:4, 0:65]
        dst = W["STF"].ap().rearrange("p (pr e) -> p pr e", e=65)[:, 4 * bi:4 * bi + 4, :]
        p.emit("v", v.tensor_tensor(out=dst, in0=src, in1=dst, op=ALU.add))
    p.fence("v")
    state_cast(nc, p, sb)


def epilogue(nc, p, sb, L):
    s = p.pre("s")
    s.wait_ge(p.dsem, 16 * p.dcnt)
    if p.dcnt_g:
        s.wait_ge(p.dsem_g, 16 * p.dcnt_g)


_NC_CACHE = {}


def kernel(L0_attn_x, L0_attn_w_qkv, L0_attn_w_out, L0_attn_ln_g, L0_attn_ln_b,
           L0_attn_mem_beta, L0_ffn_rhs):
    from concourse.bass_utils import run_bass_kernel_spmd
    x = np.asarray(L0_attn_x, dtype=np.float32)
    base = make_inputs(x[0], L0_attn_w_qkv, L0_attn_w_out, L0_attn_ln_g,
                       L0_attn_ln_b, L0_attn_mem_beta, L0_ffn_rhs)
    in_maps = []
    for bidx in range(8):
        m = dict(base)
        m["x"] = np.ascontiguousarray(x[bidx])
        in_maps.append(m)
    if "nc" not in _NC_CACHE:
        _NC_CACHE["nc"] = build_core(debug_dump=[], n_chunks=NCH)
    res = run_bass_kernel_spmd(_NC_CACHE["nc"], in_maps, core_ids=list(range(8)))
    out = np.stack([np.asarray(r["out"], dtype=np.float32) for r in res.results], axis=0)
    return out


# version 04
# speedup vs baseline: 1.9986x; 1.8283x over previous
import numpy as np
import contextlib
import concourse.bass as bass
import concourse.mybir as mybir

F32 = mybir.dt.float32
I32 = mybir.dt.int32
BF16 = mybir.dt.bfloat16

H, DH, SEG, D, N = 16, 64, 32, 1024, 4096
CHUNK, NCH = 256, 16


def host_constants():
    c = {}
    perm = np.concatenate([np.arange(0, 64, 2), np.arange(1, 64, 2)])
    c["perm_head"] = perm
    pos = np.arange(SEG, dtype=np.float64)
    freqs = 10000.0 ** (-np.arange(0, 64, 2, dtype=np.float64) / 64)
    ang = freqs[:, None] * pos[None, :]
    cos = np.cos(ang)
    sin = np.sin(ang)
    cos64 = np.concatenate([cos, cos], 0)
    sin64 = np.concatenate([-sin, sin], 0)
    cos128 = np.concatenate([cos64, cos64], 0)
    sin128 = np.concatenate([sin64, sin64], 0)
    c["cos_hi"] = np.ascontiguousarray(cos128.astype(np.float32))
    c["cos_lo"] = np.ascontiguousarray((cos128 - c["cos_hi"].astype(np.float64)).astype(np.float32))
    c["sin_hi"] = np.ascontiguousarray(sin128.astype(np.float32))
    c["sin_lo"] = np.ascontiguousarray((sin128 - c["sin_hi"].astype(np.float64)).astype(np.float32))
    kt = np.arange(128)[:, None]
    qt = np.arange(128)[None, :]
    allowed = (kt // SEG == qt // SEG) & (kt <= qt)
    c["maska"] = np.ascontiguousarray(np.where(allowed, 0.0, -1e30).astype(np.float32))
    import ml_dtypes
    c["ident"] = np.ascontiguousarray(np.eye(128, dtype=np.float32).astype(ml_dtypes.bfloat16))
    c["identf"] = np.ascontiguousarray(np.eye(128, dtype=np.float32))
    return c


def permute_wqkv(w_qkv):
    perm = host_constants()["perm_head"]
    cols = np.arange(3072)
    for h in range(H):
        cols[64 * h: 64 * h + 64] = 64 * h + perm
        cols[1024 + 64 * h: 1024 + 64 * h + 64] = 1024 + 64 * h + perm
    return np.ascontiguousarray(w_qkv[:, cols])


class P:
    ENGS = ("t", "v", "a", "g", "s")

    def __init__(self, nc, stack):
        self.nc = nc
        self.sem = {e: stack.enter_context(nc.semaphore("sem_" + e)) for e in self.ENGS}
        self.dsem = stack.enter_context(nc.semaphore("sem_dma"))
        self.dsem_g = stack.enter_context(nc.semaphore("sem_dmag"))
        self.dcnt_g = 0
        self.cnt = {e: 0 for e in self.ENGS}
        self.dcnt = 0
        self.eng = {"t": nc.tensor, "v": nc.vector, "a": nc.scalar,
                    "g": nc.gpsimd, "s": nc.sync}
        self.last = {e: None for e in self.ENGS}
        self.need_wait = {e: {} for e in self.ENGS}

    def emit(self, e, instr):
        if instr is not None:
            self.last[e] = instr
        return instr

    def _flush_waits(self, e):
        nw = self.need_wait[e]
        for o, cval in sorted(nw.items()):
            if o == "d":
                self.eng[e].wait_ge(self.dsem, 16 * cval)
            elif o == "dg":
                self.eng[e].wait_ge(self.dsem_g, 16 * cval)
            else:
                self.eng[e].wait_ge(self.sem[o], cval)
        self.need_wait[e] = {}

    def pre(self, e):
        self._flush_waits(e)
        return self.eng[e]

    def fence(self, *engines, waiters=None):
        for e in engines:
            if self.last[e] is not None:
                self.last[e].then_inc(self.sem[e], 1)
                self.cnt[e] += 1
                self.last[e] = None
        for w in (self.ENGS if waiters is None else waiters):
            for e in engines:
                self.need_wait[w][e] = self.cnt[e]
        return self.cnt[engines[0]]

    def mini(self, e):
        if self.last[e] is None:
            return self.cnt[e]
        self.last[e].then_inc(self.sem[e], 1)
        self.cnt[e] += 1
        self.last[e] = None
        self.eng[e].wait_ge(self.sem[e], self.cnt[e])
        return self.cnt[e]

    def wait_cnt(self, w, e, cval):
        if cval > 0:
            cur = self.need_wait[w].get(e, 0)
            self.need_wait[w][e] = max(cur, cval)

    def wait_dma(self, w, cval):
        if cval > 0:
            cur = self.need_wait[w].get("d", 0)
            self.need_wait[w]["d"] = max(cur, cval)

    def dma(self, out, in_, queue="s", allow_slow=False):
        eng = self.pre(queue)
        instr = eng.dma_start(out=out, in_=in_, allow_slow_non_contiguous=allow_slow)
        if queue == "g":
            instr.then_inc(self.dsem_g, 16)
            self.dcnt_g += 1
        else:
            instr.then_inc(self.dsem, 16)
            self.dcnt += 1
        return instr

    def barrier_dma(self):
        for w in self.ENGS:
            self.need_wait[w]["d"] = self.dcnt
            if self.dcnt_g:
                self.need_wait[w]["dg"] = self.dcnt_g


BUFS = [
    ("WQKV", (128, 8 * 3072), BF16),
    ("WOUT", (128, 8 * 1024), BF16),
    ("QRHS", (128, 8 * 1024), BF16),
    ("RSB", (128, 1024), F32),
    ("INVSB", (128, 1024), F32),
    ("VBIASB", (128, 1024), F32),
    ("GCOL", (128, 8), F32),
    ("BCOL", (128, 8), F32),
    ("BIASF", (128, 16), F32),
    ("BVROWF", (1, 1024), F32),
    ("ONER", (1, 128), BF16),
    ("ONESF", (1, 128), F32),
    ("ONESH", (128, 128), BF16),
    ("VBIASH", (128, 1024), BF16),
    ("BETA", (128, 16), F32),
    ("BETB", (128, 16), F32),
    ("COS2", (128, 32), BF16),
    ("SIN2", (128, 32), BF16),
    ("MASKA", (128, 128), F32),
    ("IDENT", (128, 128), BF16),
    ("IDENTF", (128, 128), F32),
    ("STF", (128, 520), F32),
    ("STBP", (128, 1040), BF16),
    ("C64", (128, 64), F32),
    ("C64B", (128, 64), BF16),
    ("RUF", (128, 16), F32),
    ("RQF", (128, 16), F32),
    ("XT", (128, 2048), F32),
    ("X1", (128, 2048), F32),
    ("XN2", (128, 2048), F32),
    ("FOUT", (128, 2048), F32),
    ("XN", (128, 2048), BF16),
    ("XNT", (128, 2048), BF16),
    ("QN", (128, 2048), BF16),
    ("KN", (128, 2048), BF16),
    ("QR", (128, 2048), BF16),
    ("KR", (128, 2048), BF16),
    ("SQ", (128, 2048), BF16),
    ("SK", (128, 2048), BF16),
    ("VB", (128, 2080), BF16),
    ("GHB", (128, 4096), BF16),
    ("UB", (128, 1040), BF16),
    ("TBF", (128, 1040), BF16),
    ("OT", (128, 2048), BF16),
    ("QI", (128, 1024), I32),
    ("TQ", (128, 1024), BF16),
    ("TMPA", (128, 1024), BF16),
    ("TMPB", (128, 1024), BF16),
    ("FTMP", (128, 1024), F32),
    ("ROWF", (8, 1024), F32),
]
ALIASES = {"QLTOK": "XN", "QLT": "OT", "ROWS": "FTMP", "JUNK": "TQ",
           "OB": "XNT", "SKT": "KN", "EB": "QN"}


def build_core(debug_dump=(), n_chunks=NCH):
    nc = bass.Bass("TRN2", target_bir_lowering=False)
    L = {}
    L["dx"] = nc.dram_tensor("x", [N, D], F32, kind="ExternalInput").ap()
    L["dwqkv"] = nc.dram_tensor("wqkv", [D, 3072], F32, kind="ExternalInput").ap()
    L["dwout"] = nc.dram_tensor("wout", [D, D], F32, kind="ExternalInput").ap()
    L["dg"] = nc.dram_tensor("ln_g", [D], F32, kind="ExternalInput").ap()
    L["db"] = nc.dram_tensor("ln_b", [D], F32, kind="ExternalInput").ap()
    L["dbeta"] = nc.dram_tensor("mem_beta", [H], F32, kind="ExternalInput").ap()
    L["drhs"] = nc.dram_tensor("rhs", [D, D], F32, kind="ExternalInput").ap()
    L["dch"] = nc.dram_tensor("cos_hi", [128, 32], F32, kind="ExternalInput").ap()
    L["dcl"] = nc.dram_tensor("cos_lo", [128, 32], F32, kind="ExternalInput").ap()
    L["dsh"] = nc.dram_tensor("sin_hi", [128, 32], F32, kind="ExternalInput").ap()
    L["dsl"] = nc.dram_tensor("sin_lo", [128, 32], F32, kind="ExternalInput").ap()
    L["dmask"] = nc.dram_tensor("maska", [128, 128], F32, kind="ExternalInput").ap()
    L["dident"] = nc.dram_tensor("ident", [128, 128], BF16, kind="ExternalInput").ap()
    L["didentf"] = nc.dram_tensor("identf", [128, 128], F32, kind="ExternalInput").ap()
    L["dout"] = nc.dram_tensor("out", [N, D], F32, kind="ExternalOutput").ap()
    L["ddbg"] = {}
    L["debug_dump"] = list(debug_dump)
    L["n_chunks"] = n_chunks
    L["dump_psum"] = any(n == "PSA" for n, _, _, _ in debug_dump)
    L["dump_comb"] = any(n == "TMA" for n, _, _, _ in debug_dump)
    for name, buf, shape, dt in debug_dump:
        L["ddbg"][name] = nc.dram_tensor("dbg_" + name, list(shape), dt,
                                         kind="ExternalOutput").ap()

    with contextlib.ExitStack() as st_:
        sb = {}
        for name, shape, dt in BUFS:
            sb[name] = st_.enter_context(nc.sbuf_tensor(name, list(shape), dt))
        for al, tgt in ALIASES.items():
            sb[al] = sb[tgt]
        for nm in ("pa0", "pa1", "pb0", "pb1", "pp0", "pp1", "pp2"):
            sb[nm] = st_.enter_context(nc.psum_tensor(nm, [128, 512], F32))
        sb["ptr"] = st_.enter_context(nc.psum_tensor("ptr", [128, 1024], BF16))
        p = P(nc, st_)
        with nc.Block() as block:
            @block.sync
            def _(sync):
                prologue(nc, p, sb, L)
                for c in range(n_chunks):
                    chunk_stages(nc, p, sb, L, c)
                epilogue(nc, p, sb, L)
    return nc


def make_inputs(x_b, w_qkv, w_out, ln_g, ln_b, mem_beta, rhs):
    c = host_constants()
    return {
        "x": np.ascontiguousarray(x_b.astype(np.float32)),
        "wqkv": permute_wqkv(np.asarray(w_qkv, dtype=np.float32)),
        "wout": np.ascontiguousarray(np.asarray(w_out, dtype=np.float32)),
        "ln_g": np.ascontiguousarray(np.asarray(ln_g, dtype=np.float32)),
        "ln_b": np.ascontiguousarray(np.asarray(ln_b, dtype=np.float32)),
        "mem_beta": np.ascontiguousarray(np.asarray(mem_beta, dtype=np.float32).reshape(H)),
        "rhs": np.ascontiguousarray(np.asarray(rhs, dtype=np.float32)),
        "cos_hi": c["cos_hi"], "cos_lo": c["cos_lo"],
        "sin_hi": c["sin_hi"], "sin_lo": c["sin_lo"],
        "maska": c["maska"], "ident": c["ident"], "identf": c["identf"],
    }


AF = mybir.ActivationFunctionType
ALU = mybir.AluOpType
NTT, NBL, NF = 2, 2, 8





RHSBUFS = [("XT", 0), ("XT", 1), ("X1", 0), ("X1", 1),
           ("XN2", 0), ("XN2", 1), ("FOUT", 0), ("FOUT", 1)]

PNBANKS = [("pp0", 7, 0), ("pp1", 7, 0), ("pp2", 2, 0)]
SDBANKS = [("pa0", 7, 0), ("pa1", 7, 0), ("pp2", 2, 260)]


def pn_slot(h):
    bank, nh, base = PNBANKS[min(h // 7, 2)]
    return bank, base + 65 * (h - 7 * min(h // 7, 2))


def sd_slot(h):
    bank, nh, base = SDBANKS[min(h // 7, 2)]
    return bank, base + 65 * (h - 7 * min(h // 7, 2))


def a1(ap):
    kk = ap.shape[-1]
    return ap.rearrange("p (a k) -> p a k", a=1)


def bc_inner(ap_2d, n_inner):
    P_, K_ = ap_2d.shape
    return ap_2d.rearrange("p (k a) -> p k a", a=1).broadcast_to([P_, K_, n_inner])




def bcast_row(nc, p, sb, src_row, dst, n, bank_name="pa0"):
    W = sb
    for c0 in range(0, n, 512):
        w = min(512, n - c0)
        t = p.pre("t")
        p.emit("t", t.matmul(W[bank_name][:, 0:w], W["ONESF"][:, :],
                             src_row[:, c0:c0 + w], start=True, stop=True,
                             skip_group_check=True))
        p.fence("t")
        v = p.pre("v")
        p.emit("v", v.tensor_copy(out=dst[:, c0:c0 + w], in_=W[bank_name][:, 0:w]))
        p.fence("v")


def prologue(nc, p, sb, L):
    W = sb
    for i in range(8):
        p.dma(out=W["WQKV"][:, 3072 * i:3072 * (i + 1)],
              in_=L["dwqkv"][128 * i:128 * (i + 1), :], queue="g")
        p.dma(out=W["WOUT"][:, 1024 * i:1024 * (i + 1)],
              in_=L["dwout"][128 * i:128 * (i + 1), :], queue="g")
    p.dma(out=W["GCOL"][:, :], in_=L["dg"].rearrange("(i p) -> p i", p=128), queue="s", allow_slow=True)
    p.dma(out=W["BCOL"][:, :], in_=L["db"].rearrange("(i p) -> p i", p=128), queue="s", allow_slow=True)
    p.dma(out=W["C64"][0:1, 0:16], in_=L["dbeta"].rearrange("(a h) -> a h", a=1), queue="s")
    p.dma(out=W["MASKA"][:, :], in_=L["dmask"], queue="s")
    p.dma(out=W["IDENT"][:, :], in_=L["dident"], queue="s")
    p.dma(out=W["IDENTF"][:, :], in_=L["didentf"], queue="s")
    p.dma(out=W["FTMP"][:, 0:32], in_=L["dch"], queue="s")
    p.dma(out=W["FTMP"][:, 32:64], in_=L["dcl"], queue="s")
    p.dma(out=W["FTMP"][:, 64:96], in_=L["dsh"], queue="s")
    p.dma(out=W["FTMP"][:, 96:128], in_=L["dsl"], queue="s")
    for i, (bn, half) in enumerate(RHSBUFS):
        p.dma(out=W[bn][:, 1024 * half:1024 * (half + 1)],
              in_=L["drhs"][128 * i:128 * (i + 1), :], queue="s")
    p.barrier_dma()

    v = p.pre("v")
    p.emit("v", v.tensor_tensor(out=W["COS2"][:, :], in0=W["FTMP"][:, 0:32],
                                in1=W["FTMP"][:, 32:64], op=ALU.add))
    p.emit("v", v.tensor_tensor(out=W["SIN2"][:, :], in0=W["FTMP"][:, 64:96],
                                in1=W["FTMP"][:, 96:128], op=ALU.add))
    p.emit("v", v.tensor_copy(out=W["C64B"][:, 0:8], in_=W["BCOL"][:, :]))
    p.emit("v", v.memset(W["ONESF"][:, :], 1.0))
    p.fence("v")
    a = p.pre("a")
    p.emit("a", a.activation(out=W["C64"][0:1, 16:32], in_=W["C64"][0:1, 0:16],
                             func=AF.Sigmoid))
    p.fence("a")
    bcast_row(nc, p, sb, W["C64"][0:1, 16:32], W["BETA"][:, :], 16)
    v = p.pre("v")
    p.emit("v", v.tensor_scalar(out=W["BETB"][:, :], in0=W["BETA"][:, :], scalar1=-1.0,
                                scalar2=1.0, op0=ALU.mult, op1=ALU.add))
    p.fence("v")

    t = p.pre("t")
    for j in range(16):
        for i in range(8):
            p.emit("t", t.matmul(W["pa0"][:, j:j + 1],
                                 W["WQKV"][:, 3072 * i + 128 * j:3072 * i + 128 * (j + 1)],
                                 W["C64B"][:, i:i + 1],
                                 start=(i == 0 and j == 0), stop=(i == 7),
                                 skip_group_check=True))
    for nch in range(2):
        bank = W["pa1"] if nch == 0 else W["pb0"]
        for i in range(8):
            c0 = 3072 * i + 2048 + 512 * nch
            p.emit("t", t.matmul(bank[0:1, 0:512],
                                 W["C64B"][:, i:i + 1], W["WQKV"][:, c0:c0 + 512],
                                 start=(i == 0), stop=(i == 7),
                                 skip_group_check=True))
    p.fence("t")
    v = p.pre("v")
    p.emit("v", v.tensor_copy(out=W["BIASF"][:, :], in_=W["pa0"][:, 0:16]))
    p.emit("v", v.tensor_copy(out=W["BVROWF"][:, 0:512], in_=W["pa1"][0:1, 0:512]))
    p.emit("v", v.tensor_copy(out=W["BVROWF"][:, 512:1024], in_=W["pb0"][0:1, 0:512]))
    for i in range(8):
        p.emit("v", v.tensor_scalar(out=W["WQKV"][:, 3072 * i:3072 * (i + 1)],
                                    in0=W["WQKV"][:, 3072 * i:3072 * (i + 1)],
                                    scalar1=W["GCOL"][:, i:i + 1], scalar2=None,
                                    op0=ALU.mult, op1=ALU.bypass))
    p.fence("v")
    bcast_row(nc, p, sb, W["BVROWF"][0:1, :], W["VBIASB"][:, :], 1024, bank_name="pa1")
    v = p.pre("v")
    p.emit("v", v.memset(W["ONESH"][:, :], 0.0078125))
    p.emit("v", v.tensor_copy(out=W["VBIASH"][:, :], in_=W["VBIASB"][:, :]))
    p.fence("v")

    for i in range(8):
        bn, half = RHSBUFS[i]
        t = p.pre("t")
        for j in range(8):
            bank = W["pb0"] if j < 4 else W["pb1"]
            p.emit("t", t.transpose(bank[:, 128 * (j % 4):128 * (j % 4 + 1)],
                                    sb[bn][:, 1024 * half + 128 * j:1024 * half + 128 * (j + 1)],
                                    W["IDENTF"][:, :]))
        p.fence("t")
        v = p.pre("v")
        for j in range(8):
            bank = W["pb0"] if j < 4 else W["pb1"]
            p.emit("v", v.tensor_reduce(out=W["C64"][:, 48 + j:49 + j],
                                        in_=bank[:, 128 * (j % 4):128 * (j % 4 + 1)],
                                        axis=mybir.AxisListType.X, op=ALU.max,
                                        apply_absolute_value=True))
        p.mini("v")
        if i == 0:
            p.emit("v", v.tensor_copy(out=W["C64"][:, 32:40], in_=W["C64"][:, 48:56]))
        else:
            p.emit("v", v.tensor_tensor(out=W["C64"][:, 32:40], in0=W["C64"][:, 32:40],
                                        in1=W["C64"][:, 48:56], op=ALU.max))
        p.mini("v")
        p.fence("v")
    v = p.pre("v")
    p.emit("v", v.tensor_scalar(out=W["C64"][:, 32:40], in0=W["C64"][:, 32:40],
                                scalar1=1e-12, scalar2=None, op0=ALU.max, op1=ALU.bypass))
    p.fence("v")
    t = p.pre("t")
    p.emit("t", t.transpose(W["pb0"][0:8, 0:128], W["C64"][:, 32:40], W["IDENTF"][:, :]))
    p.fence("t")
    v = p.pre("v")
    p.emit("v", v.tensor_copy(out=W["ROWF"][0:8, 0:128], in_=W["pb0"][0:8, 0:128]))
    p.fence("v")
    for j in range(8):
        p.dma(out=W["RSB"][0:1, 128 * j:128 * (j + 1)],
              in_=W["ROWF"][j:j + 1, 0:128], queue="s")
    p.barrier_dma()
    v = p.pre("v")
    p.emit("v", v.reciprocal(out=W["INVSB"][0:1, 0:1024], in_=W["RSB"][0:1, 0:1024]))
    p.mini("v")
    p.emit("v", v.tensor_scalar(out=W["INVSB"][0:1, 0:1024], in0=W["INVSB"][0:1, 0:1024],
                                scalar1=127.0, scalar2=None, op0=ALU.mult, op1=ALU.bypass))
    p.emit("v", v.memset(W["C64"][:, 49:50], 127.0))
    p.mini("v")
    p.emit("v", v.reciprocal(out=W["C64"][:, 50:51], in_=W["C64"][:, 49:50]))
    p.mini("v")
    p.emit("v", v.tensor_scalar(out=W["RSB"][0:1, 0:1024], in0=W["RSB"][0:1, 0:1024],
                                scalar1=W["C64"][0:1, 50:51], scalar2=None,
                                op0=ALU.mult, op1=ALU.bypass))
    p.fence("v")
    bcast_row(nc, p, sb, W["INVSB"][0:1, 0:1024], W["INVSB"][:, :], 1024, bank_name="pa0")
    bcast_row(nc, p, sb, W["RSB"][0:1, 0:1024], W["RSB"][:, :], 1024, bank_name="pa1")
    for i in range(8):
        bn, half = RHSBUFS[i]
        src = sb[bn][:, 1024 * half:1024 * (half + 1)]
        quantize_tile(nc, p, sb, src_f32=src, invs=W["INVSB"][:, :],
                      dst=W["QRHS"][:, 1024 * i:1024 * (i + 1)], per_partition=False)

    v = p.pre("v")
    p.emit("v", v.memset(W["STF"][:, :], 0.0))
    p.emit("v", v.memset(W["STBP"][:, :], 0.0))
    p.emit("v", v.memset(W["UB"][:, :], 0.0))
    p.mini("v")
    stf3 = W["STF"].ap().rearrange("p (pr c) -> p pr c", c=65)
    p.emit("v", v.memset(stf3[:, :, 64:65], 1e-6))
    p.fence("v")
    state_cast(nc, p, sb)
    v = p.pre("v")
    vb4 = W["VB"].ap().rearrange("p (t2 h c) -> p t2 h c", t2=2, c=65)
    p.emit("v", v.memset(vb4[:, :, :, 64:65], 1.0))
    p.fence("v")


def state_cast(nc, p, sb):
    W = sb
    stf3 = W["STF"].ap().rearrange("p (pr c) -> p pr c", c=65)
    stbp2 = W["STBP"].ap().rearrange("p (m r) -> p m r", r=130)
    v = p.pre("v")
    p.emit("v", v.tensor_copy(out=stbp2[0:64, :, 0:65], in_=stf3[0:64, :, :]))
    p.emit("v", v.tensor_copy(out=stbp2[64:128, :, 65:130], in_=stf3[64:128, :, :]))
    p.fence("v")


def quantize_tile(nc, p, sb, src_f32, invs, dst, per_partition, invs_col=None, scr=None):
    W = sb
    if scr is None:
        scr = W["FTMP"][:, 0:1024]
    v = p.pre("v")
    p.mini("v")
    if per_partition:
        p.emit("v", v.tensor_scalar(out=scr, in0=src_f32, scalar1=invs_col,
                                    scalar2=None, op0=ALU.mult, op1=ALU.bypass))
    else:
        p.emit("v", v.tensor_tensor(out=scr, in0=src_f32, in1=invs,
                                    op=ALU.mult))
    p.fence("v", waiters=("a",))
    a = p.pre("a")
    p.emit("a", a.activation(out=W["QI"][:, 0:1024], in_=scr, func=AF.Copy))
    p.fence("a", waiters=("v",))
    v = p.pre("v")
    p.emit("v", v.scalar_tensor_tensor(out=W["TQ"][:, 0:1024], in0=scr,
                                       scalar=1.0, in1=W["QI"][:, 0:1024],
                                       op0=ALU.mult, op1=ALU.subtract))
    p.mini("v")
    p.emit("v", v.scalar_tensor_tensor(out=W["TMPA"][:, :], in0=W["TQ"][:, 0:1024],
                                       scalar=0.5, in1=W["QI"][:, 0:1024],
                                       op0=ALU.is_gt, op1=ALU.add))
    p.mini("v")
    p.emit("v", v.scalar_tensor_tensor(out=W["TMPB"][:, :], in0=W["TQ"][:, 0:1024],
                                       scalar=-0.5, in1=W["TMPA"][:, :],
                                       op0=ALU.is_ge, op1=ALU.add))
    p.mini("v")
    p.emit("v", v.tensor_scalar(out=dst, in0=W["TMPB"][:, :], scalar1=-1.0,
                                scalar2=None, op0=ALU.add, op1=ALU.bypass))
    return p.fence("v", waiters=("t", "a"))


def ln_fused(nc, p, sb, src_ap, dst_ap, cbase, double, apply_waiters=None):
    W = sb
    C = W["C64"]
    v = p.pre("v")
    p.emit("v", v.bn_stats(out=C[:, cbase:cbase + 6], in_=src_ap[:, 0:512]))
    p.emit("v", v.bn_stats(out=C[:, cbase + 6:cbase + 12], in_=src_ap[:, 512:1024]))
    p.mini("v")
    p.emit("v", v.bn_aggr(out=C[:, cbase + 12:cbase + 14], in_=C[:, cbase:cbase + 12]))
    p.mini("v")
    m1 = C[:, cbase + 12:cbase + 13]
    v1 = C[:, cbase + 13:cbase + 14]
    A = C[:, cbase:cbase + 1]
    if double:
        p.emit("v", v.tensor_scalar(out=A, in0=v1, scalar1=1.00001,
                                    scalar2=1.0000100000000001e-05,
                                    op0=ALU.mult, op1=ALU.add))
    else:
        p.emit("v", v.tensor_scalar(out=A, in0=v1, scalar1=1e-5, scalar2=None,
                                    op0=ALU.add, op1=ALU.bypass))
    p.fence("v", waiters=("a", "v"))
    a = p.pre("a")
    p.emit("a", a.activation(out=C[:, cbase + 5:cbase + 6], in_=A, func=AF.Sqrt))
    p.fence("a", waiters=("v", "a"))
    v = p.pre("v")
    p.emit("v", v.reciprocal(out=C[:, cbase + 7:cbase + 8], in_=C[:, cbase + 5:cbase + 6]))
    p.mini("v")
    R = C[:, cbase + 7:cbase + 8]
    p.emit("v", v.tensor_scalar(out=C[:, cbase + 9:cbase + 10], in0=m1, scalar1=R,
                                scalar2=-1.0, op0=ALU.mult, op1=ALU.mult))
    p.fence("v", waiters=("a",))
    if dst_ap is not None:
        a = p.pre("a")
        p.emit("a", a.activation(out=dst_ap, in_=src_ap, func=AF.Identity,
                                 bias=C[:, cbase + 9:cbase + 10], scale=R))
        p.fence("a", waiters=apply_waiters)
    return R, C[:, cbase + 9:cbase + 10]


def chunk_stages(nc, p, sb, L, c):
    W = sb
    if c == 0:
        p.dma(out=W["XT"].ap().rearrange("p (a d) -> p a d", a=2),
              in_=L["dx"][CHUNK * c:CHUNK * (c + 1), :].rearrange("(a p) d -> p a d", p=128),
              queue="s")
        p.barrier_dma()
    else:
        p.wait_dma("v", L["xcnt"][c])
        p.wait_dma("a", L["xcnt"][c])

    for tt in range(NTT):
        ln_fused(nc, p, sb, W["XT"][:, 1024 * tt:1024 * (tt + 1)],
                 W["XN"][:, 1024 * tt:1024 * (tt + 1)], cbase=16 * tt, double=True)

    for tt in range(NTT):
        t = p.pre("t")
        for f in range(8):
            p.emit("t", t.transpose(W["ptr"][:, 128 * f:128 * (f + 1)],
                                    W["XN"][:, 1024 * tt + 128 * f:1024 * tt + 128 * (f + 1)],
                                    W["IDENT"][:, :]))
        p.fence("t")
        a = p.pre("a")
        p.emit("a", a.activation(
            out=W["XNT"].ap().rearrange("p (f b) -> p f b", b=256)[:, :, 128 * tt:128 * (tt + 1)],
            in_=W["ptr"][:, :], func=AF.Copy))
        p.fence("a")

    cosb4 = a1(W["COS2"][:, :]).broadcast_to([128, 32, 32])
    sinb4 = a1(W["SIN2"][:, :]).broadcast_to([128, 32, 32])
    evac_cnt = {}

    def qkv_j(j):
        bank = W["pa0"] if j % 2 == 0 else W["pa1"]
        col = 256 * ((j // 2) % 2)
        ps = bank[:, col:col + 256]
        if j >= 4:
            p.wait_cnt("t", "a", evac_cnt[j - 4])
        t = p.pre("t")
        for i in range(8):
            p.emit("t", t.matmul(ps, W["WQKV"][:, 3072 * i + 128 * j:3072 * i + 128 * (j + 1)],
                                 W["XNT"][:, 256 * i:256 * (i + 1)],
                                 start=(i == 0), stop=(i == 7), skip_group_check=True))
        p.fence("t", waiters=("a",))
        a = p.pre("a")
        dstbuf = W["QN"] if j < 8 else W["KN"]
        pf = j % 8
        p.emit("a", a.activation(out=dstbuf[:, 256 * pf:256 * (pf + 1)], in_=ps,
                                 func=AF.Identity, bias=W["BIASF"][:, j:j + 1],
                                 scale=np.float32(1.0)))
        evac_cnt[j] = p.fence("a", waiters=())

    def rope_half(srcb, dstb, tm, rb, hf, wait_a):
        cols = slice(1024 * hf, 1024 * (hf + 1))
        src_ = W[srcb][:, cols]
        p.wait_cnt("v", "a", wait_a)
        e = p.pre("v")
        p.emit("v", e.tensor_copy(out=W[rb][0:32, 0:1024], in_=src_[32:64, :]))
        p.emit("v", e.tensor_copy(out=W[rb][32:64, 0:1024], in_=src_[0:32, :]))
        p.emit("v", e.tensor_copy(out=W[rb][64:96, 0:1024], in_=src_[96:128, :]))
        p.emit("v", e.tensor_copy(out=W[rb][96:128, 0:1024], in_=src_[64:96, :]))
        p.mini("v")
        p.emit("v", e.tensor_tensor(out=W[tm][:, :].rearrange("p (r k) -> p r k", k=32),
                                    in0=src_.rearrange("p (r k) -> p r k", k=32),
                                    in1=cosb4, op=ALU.mult))
        p.mini("v")
        p.emit("v", e.tensor_tensor(out=W[rb][:, 0:1024].rearrange("p (r k) -> p r k", k=32),
                                    in0=W[rb][:, 0:1024].rearrange("p (r k) -> p r k", k=32),
                                    in1=sinb4, op=ALU.mult))
        p.mini("v")
        p.emit("v", e.tensor_tensor(out=W[dstb][:, cols], in0=W[tm][:, :],
                                    in1=W[rb][:, 0:1024], op=ALU.add))
        p.fence("v", waiters=())

    for j in range(8):
        qkv_j(j)
    rope_half("QN", "QR", "TMPA", "TBF", 0, evac_cnt[3])
    for j in range(8, 12):
        qkv_j(j)
    rope_half("QN", "QR", "TMPA", "TBF", 1, evac_cnt[7])
    for j in range(12, 16):
        qkv_j(j)

    vb4 = W["VB"].ap().rearrange("p (t2 h cc) -> p t2 h cc", t2=2, cc=65)
    vcnt = {}
    for gidx, (tt, nch) in enumerate([(a_, b_) for a_ in range(NTT) for b_ in range(2)]):
        ps = (W["pb0"] if nch == 0 else W["pb1"])[:, 0:512]
        if gidx >= 2:
            p.wait_cnt("t", "a", vcnt[gidx - 2])
        t = p.pre("t")
        for i in range(8):
            c0 = 3072 * i + 2048 + 512 * nch
            p.emit("t", t.matmul(ps, W["XNT"][:, 256 * i + 128 * tt:256 * i + 128 * (tt + 1)],
                                 W["WQKV"][:, c0:c0 + 512],
                                 start=(i == 0), stop=False, skip_group_check=True))
        p.emit("t", t.matmul(ps, W["ONESH"][:, :],
                             W["VBIASH"][:, 512 * nch:512 * (nch + 1)],
                             start=False, stop=True, skip_group_check=True))
        tcnt = p.fence("t", waiters=("a",))
        p.wait_cnt("a", "t", tcnt)
        a = p.pre("a")
        p.emit("a", a.activation(out=vb4[:, tt, 8 * nch:8 * (nch + 1), 0:64],
                                 in_=ps.rearrange("p (h e) -> p h e", e=64), func=AF.Copy))
        vcnt[gidx] = p.fence("a", waiters=(("v", "t") if gidx == 3 else ()))
        if gidx == 0:
            rope_half("KN", "KR", "TMPB", "UB", 0, evac_cnt[11])
        elif gidx == 1:
            rope_half("KN", "KR", "TMPB", "UB", 1, evac_cnt[15])

    def elu_pair(srcb, dstb, wait_v):
        p.wait_cnt("a", "v", wait_v)
        a = p.pre("a")
        p.emit("a", a.activation(out=W["TMPA"][:, :], in_=W[srcb][:, 0:1024],
                                 func=AF.Exp))
        p.emit("a", a.activation(out=W["TMPB"][:, :], in_=W[srcb][:, 1024:2048],
                                 func=AF.Exp))
        p.fence("a", waiters=("v",))
        v = p.pre("v")
        p.emit("v", v.tensor_scalar(out=W["TMPA"][:, :], in0=W["TMPA"][:, :],
                                    scalar1=1.0, scalar2=None, op0=ALU.min, op1=ALU.bypass))
        p.emit("v", v.tensor_scalar(out=W["TMPB"][:, :], in0=W["TMPB"][:, :],
                                    scalar1=1.0, scalar2=None, op0=ALU.min, op1=ALU.bypass))
        p.mini("v")
        p.emit("v", v.scalar_tensor_tensor(out=W[dstb][:, 0:1024], in0=W[srcb][:, 0:1024],
                                           scalar=1.0, in1=W["TMPA"][:, :],
                                           op0=ALU.add, op1=ALU.max))
        p.emit("v", v.scalar_tensor_tensor(out=W[dstb][:, 1024:2048], in0=W[srcb][:, 1024:2048],
                                           scalar=1.0, in1=W["TMPB"][:, :],
                                           op0=ALU.add, op1=ALU.max))
        return p.fence("v", waiters=("a",))

    vc_rope = p.cnt["v"]
    elu_pair("QN", "SQ", vc_rope)
    vc_elu = elu_pair("KN", "SK", 0)

    entry = {"v_rope": vc_rope, "v_elu": vc_elu,
             "a_qkv": evac_cnt[15], "a_vmm": vcnt[3]}
    ab = None
    for bl in range(NBL):
        ab = attn_block(nc, p, sb, L, c, bl, ab, entry)

    otev = {}
    for tt in range(NTT):
        if tt == 0:
            p.wait_cnt("t", "v", ab["vc_comb"])
            p.wait_cnt("t", "a", L["ac_skt"])
        else:
            p.wait_cnt("t", "a", otev[0])
        t = p.pre("t")
        for f in range(8):
            p.emit("t", t.transpose(W["ptr"][:, 128 * f:128 * (f + 1)],
                                    W["OB"][:, 1024 * tt + 128 * f:1024 * tt + 128 * (f + 1)],
                                    W["IDENT"][:, :]))
        p.fence("t", waiters=("a",))
        a = p.pre("a")
        p.emit("a", a.activation(
            out=W["OT"].ap().rearrange("p (f b) -> p f b", b=256)[:, :, 128 * tt:128 * (tt + 1)],
            in_=W["ptr"][:, :], func=AF.Copy))
        otev[tt] = p.fence("a", waiters=())

    wev = {}
    wm = {}
    for tt in range(NTT):
        p.wait_cnt("t", "a", otev[tt])
        if tt == 0:
            p.wait_cnt("t", "v", ab["vc_stf"])
        else:
            p.wait_cnt("t", "a", wev[0])
        for nch in range(2):
            ps = (W["pb0"] if nch == 0 else W["pb1"])[:, 0:512]
            t = p.pre("t")
            for i in range(8):
                p.emit("t", t.matmul(ps, W["OT"][:, 256 * i + 128 * tt:256 * i + 128 * (tt + 1)],
                                     W["WOUT"][:, 1024 * i + 512 * nch:1024 * i + 512 * (nch + 1)],
                                     start=(i == 0), stop=(i == 7), skip_group_check=True))
            wm[tt] = p.fence("t", waiters=("a",))
            a = p.pre("a")
            dstw = (W["FTMP"][:, 512 * nch:512 * (nch + 1)] if tt == 0
                    else W["XN2"][:, 1024 + 512 * nch:1024 + 512 * (nch + 1)])
            p.emit("a", a.activation(out=dstw, in_=ps, func=AF.Copy))
            wev[tt] = p.fence("a", waiters=("v",) if nch == 1 else ())
        wsrc = W["FTMP"][:, 0:1024] if tt == 0 else W["XN2"][:, 1024:2048]
        R, NMR = ln_fused(nc, p, sb, wsrc, None, cbase=16 * tt, double=False)
        a = p.pre("a")
        tmpd = W["TMPA"] if tt == 0 else W["TMPB"]
        p.emit("a", a.activation(out=tmpd[:, :], in_=wsrc,
                                 func=AF.Identity, bias=NMR, scale=R))
        p.fence("a", waiters=("v",))
        v = p.pre("v")
        p.emit("v", v.tensor_tensor(out=W["X1"][:, 1024 * tt:1024 * (tt + 1)],
                                    in0=W["XT"][:, 1024 * tt:1024 * (tt + 1)],
                                    in1=tmpd[:, :], op=ALU.add))
        p.fence("v", waiters=(("s",) if tt == 1 else ()))

    if c + 1 < L["n_chunks"]:
        p.dma(out=W["XT"].ap().rearrange("p (a d) -> p a d", a=2),
              in_=L["dx"][CHUNK * (c + 1):CHUNK * (c + 2), :].rearrange("(a p) d -> p a d", p=128),
              queue="s")
        L.setdefault("xcnt", {})[c + 1] = p.dcnt

    lnb1 = {0: 32, 1: 0}
    lnb2 = {0: 16, 1: 32}
    amc = {0: 44, 1: 52}
    qev = {}
    deq = {}
    ap2 = {}

    def ffn_ln_quant(tt):
        am = amc[tt]
        ln_fused(nc, p, sb, W["X1"][:, 1024 * tt:1024 * (tt + 1)],
                 W["XN2"][:, 1024 * tt:1024 * (tt + 1)], cbase=lnb1[tt], double=False,
                 apply_waiters=("v",))
        v = p.pre("v")
        p.emit("v", v.tensor_reduce(out=W["C64"][:, am:am + 1], in_=W["XN2"][:, 1024 * tt:1024 * (tt + 1)],
                                    axis=mybir.AxisListType.X, op=ALU.max,
                                    apply_absolute_value=True))
        p.mini("v")
        p.emit("v", v.tensor_scalar(out=W["C64"][:, am + 1:am + 2], in0=W["C64"][:, am:am + 1],
                                    scalar1=1e-12, scalar2=None, op0=ALU.max, op1=ALU.bypass))
        p.mini("v")
        p.emit("v", v.reciprocal(out=W["C64"][:, am + 2:am + 3], in_=W["C64"][:, am + 1:am + 2]))
        p.mini("v")
        p.emit("v", v.tensor_scalar(out=W["C64"][:, am + 2:am + 3], in0=W["C64"][:, am + 2:am + 3],
                                    scalar1=127.0, scalar2=None, op0=ALU.mult, op1=ALU.bypass))
        p.emit("v", v.tensor_scalar(out=W["C64"][:, am + 3:am + 4], in0=W["C64"][:, am + 1:am + 2],
                                    scalar1=W["C64"][:, 50:51], scalar2=None,
                                    op0=ALU.mult, op1=ALU.bypass))
        p.fence("v", waiters=())
        if tt == 0 and c > 0:
            p.wait_dma("v", L["outcnt"][c - 1])
        quantize_tile(nc, p, sb, src_f32=W["XN2"][:, 1024 * tt:1024 * (tt + 1)], invs=None,
                      dst=W["QLTOK"][:, 1024 * tt:1024 * (tt + 1)], per_partition=True,
                      invs_col=W["C64"][:, am + 2:am + 3],
                      scr=W["FOUT"][:, 1024 * tt:1024 * (tt + 1)])

    def ffn_xpose(tt, wait_a):
        p.wait_cnt("t", "a", wait_a)
        t = p.pre("t")
        for f in range(8):
            p.emit("t", t.transpose(W["ptr"][:, 128 * f:128 * (f + 1)],
                                    W["QLTOK"][:, 1024 * tt + 128 * f:1024 * tt + 128 * (f + 1)],
                                    W["IDENT"][:, :]))
        p.fence("t", waiters=("a",))
        p.wait_cnt("a", "t", wm[1])
        a = p.pre("a")
        p.emit("a", a.activation(
            out=W["QLT"].ap().rearrange("p (f b) -> p f b", b=256)[:, :, 128 * tt:128 * (tt + 1)],
            in_=W["ptr"][:, :], func=AF.Copy))
        qev[tt] = p.fence("a", waiters=())

    def ffn_mm_deq(tt):
        am = amc[tt]
        p.wait_cnt("t", "a", qev[tt])
        if tt == 0:
            p.wait_cnt("t", "a", wev[1])
        else:
            p.wait_cnt("t", "v", deq[0])
        for nch in range(2):
            ps = (W["pb0"] if nch == 0 else W["pb1"])[:, 0:512]
            t = p.pre("t")
            for i in range(8):
                p.emit("t", t.matmul(ps, W["QLT"][:, 256 * i + 128 * tt:256 * i + 128 * (tt + 1)],
                                     W["QRHS"][:, 1024 * i + 512 * nch:1024 * i + 512 * (nch + 1)],
                                     start=(i == 0), stop=(i == 7), skip_group_check=True))
            p.fence("t", waiters=("v",))
        if tt == 1:
            p.wait_cnt("v", "a", ap2[0])
        v = p.pre("v")
        for nch in range(2):
            ps = (W["pb0"] if nch == 0 else W["pb1"])[:, 0:512]
            p.emit("v", v.tensor_scalar(out=W["FTMP"][:, 512 * nch:512 * (nch + 1)], in0=ps,
                                        scalar1=W["C64"][:, am + 3:am + 4], scalar2=None,
                                        op0=ALU.mult, op1=ALU.bypass))
        p.mini("v")
        p.emit("v", v.tensor_tensor(out=W["FTMP"][:, 0:1024],
                                    in0=W["FTMP"][:, 0:1024],
                                    in1=W["RSB"][:, 0:1024], op=ALU.mult))
        deq[tt] = p.fence("v", waiters=("t",))
        R, NMR = ln_fused(nc, p, sb, W["FTMP"][:, 0:1024], None, cbase=lnb2[tt], double=False)
        a = p.pre("a")
        tmpd = W["TMPA"] if tt == 0 else W["TMPB"]
        p.emit("a", a.activation(out=tmpd[:, :], in_=W["FTMP"][:, 0:1024],
                                 func=AF.Identity, bias=NMR, scale=R))
        ap2[tt] = p.fence("a", waiters=("v",))
        v = p.pre("v")
        p.emit("v", v.tensor_tensor(out=W["FOUT"][:, 1024 * tt:1024 * (tt + 1)],
                                    in0=W["X1"][:, 1024 * tt:1024 * (tt + 1)],
                                    in1=tmpd[:, :], op=ALU.add))
        p.fence("v", waiters=(("s",) if tt == 1 else ()))

    ffn_ln_quant(0)
    ffn_xpose(0, otev[1])
    ffn_ln_quant(1)
    ffn_mm_deq(0)
    ffn_xpose(1, qev[0])
    ffn_mm_deq(1)
    p.dma(out=L["dout"][CHUNK * c:CHUNK * (c + 1), :].rearrange("(a p) d -> p a d", p=128),
          in_=W["FOUT"].ap().rearrange("p (a d) -> p a d", a=2), queue="s")
    L.setdefault("outcnt", {})[c] = p.dcnt

    if c == 0 and L["debug_dump"]:
        p.barrier_dma()
        for name, buf, shape, dt in L["debug_dump"]:
            if name in ("PSA", "PSB"):
                continue
            p.dma(out=L["ddbg"][name], in_=W[buf][:, :], queue="s")
        p.barrier_dma()


def attn_block(nc, p, sb, L, c, bl, prev, entry):
    W = sb
    maskb = a1(W["MASKA"][:, :]).broadcast_to([128, 4, 128])
    eb4 = W["EB"].ap().rearrange("p (w j two q) -> p w j two q", w=2, j=4, two=2)
    ghb4 = W["GHB"].ap().rearrange("p (w j two q) -> p w j two q", w=4, j=2, two=2)

    def s_wave(wv):
        t = p.pre("t")
        for h in range(8 * wv, 8 * wv + 8):
            pf, half = h // 2, h % 2
            rows = slice(64 * half, 64 * (half + 1))
            bank = W["pa0"] if half == 0 else W["pa1"]
            slot = (h // 2) % 4
            tok = slice(256 * pf + 128 * bl, 256 * pf + 128 * (bl + 1))
            p.emit("t", t.matmul(bank[:, 128 * slot:128 * (slot + 1)],
                                 W["KR"][rows, tok], W["QR"][rows, tok],
                                 start=True, stop=True, skip_group_check=True))
        return p.fence("t", waiters=("v",))

    def gh_wave(wv):
        t = p.pre("t")
        for h in range(4 * wv, 4 * wv + 4):
            pf, half = h // 2, h % 2
            rows = slice(64 * half, 64 * (half + 1))
            bank = W["pb0"] if half == 0 else W["pb1"]
            slot = (h // 2) % 2
            tok = slice(256 * pf + 128 * bl, 256 * pf + 128 * (bl + 1))
            p.emit("t", t.matmul(bank[:, 256 * slot:256 * slot + 128],
                                 W["SK"][rows, tok], W["SQ"][rows, tok],
                                 start=True, stop=True, skip_group_check=True))
            p.emit("t", t.matmul(bank[:, 256 * slot + 128:256 * slot + 256],
                                 W["SK"][rows, tok], W["SK"][rows, tok],
                                 start=True, stop=True, skip_group_check=True))
        return p.fence("t", waiters=("a",))

    def gh_evac(wv, waiters=("t",)):
        a = p.pre("a")
        p.emit("a", a.activation(out=ghb4[:, wv, :, 0, :], in_=W["pb0"][:, :], func=AF.Copy))
        p.emit("a", a.activation(out=ghb4[:, wv, :, 1, :], in_=W["pb1"][:, :], func=AF.Copy))
        return p.fence("a", waiters=waiters)

    def premask(wv, dst, waiters=("t", "a")):
        v = p.pre("v")
        p.emit("v", v.tensor_tensor(out=dst[:, 0:512].rearrange("p (s q) -> p s q", q=128),
                                    in0=W["pa0"].ap().rearrange("p (s q) -> p s q", q=128),
                                    in1=maskb, op=ALU.add))
        p.emit("v", v.tensor_tensor(out=dst[:, 512:1024].rearrange("p (s q) -> p s q", q=128),
                                    in0=W["pa1"].ap().rearrange("p (s q) -> p s q", q=128),
                                    in1=maskb, op=ALU.add))
        return p.fence("v", waiters=waiters)

    def expev(wv, src):
        a = p.pre("a")
        p.emit("a", a.activation(out=eb4[:, wv, :, 0, :],
                                 in_=src[:, 0:512], func=AF.Exp, scale=np.float32(0.125)))
        p.emit("a", a.activation(out=eb4[:, wv, :, 1, :],
                                 in_=src[:, 512:1024], func=AF.Exp, scale=np.float32(0.125)))
        return p.fence("a", waiters=())

    def skt_xpose(tt):
        t = p.pre("t")
        for pf in range(8):
            p.emit("t", t.transpose(W["ptr"][:, 128 * pf:128 * (pf + 1)],
                                    W["SK"][:, 256 * pf + 128 * tt:256 * pf + 128 * (tt + 1)],
                                    W["IDENT"][:, :]))
        p.fence("t", waiters=("a",))
        a = p.pre("a")
        p.emit("a", a.activation(out=W["SKT"][:, 1024 * tt:1024 * (tt + 1)],
                                 in_=W["ptr"][:, :], func=AF.Copy))
        return p.fence("a", waiters=())

    if prev:
        p.wait_cnt("t", "v", prev["vc_comb"])
    else:
        p.wait_cnt("t", "v", entry["v_rope"])
        p.wait_cnt("t", "a", entry["a_qkv"])
    tc_s0 = s_wave(0)
    if prev:
        p.wait_cnt("t", "v", prev["vc_stf"])
    else:
        p.wait_cnt("t", "v", entry["v_elu"])
        p.wait_cnt("t", "a", entry["a_vmm"])
    gh_wave(0)
    gh_evac(0)
    if prev:
        p.wait_cnt("v", "a", prev["ac_exp1"])
    premask(0, W["FTMP"])
    expev(0, W["FTMP"])
    tc_s1 = s_wave(1)
    vc_p1 = premask(1, W["XN2"], waiters=("a",))
    v = p.pre("v")
    p.emit("v", v.memset(W["UB"][:, :], 0.0))
    tbf3z = W["TBF"].ap().rearrange("p (h cc) -> p h cc", cc=65)
    p.emit("v", v.memset(tbf3z[:, :, 64:65], 0.0))
    p.fence("v", waiters=())
    gh_wave(1)
    gh_evac(1)
    gh_wave(2)
    if bl == 0:
        ac_skt0 = skt_xpose(0)
    gh_evac(2)
    gh_wave(3)
    if bl == 0:
        p.wait_cnt("t", "a", ac_skt0)
        L["ac_skt"] = skt_xpose(1)
    ac_g3 = gh_evac(3, waiters=())
    ac_exp1 = expev(1, W["XN2"])

    p.wait_cnt("t", "v", vc_p1)
    t = p.pre("t")
    for h in range(H):
        pf = h // 2
        tok = slice(256 * pf + 128 * bl, 256 * pf + 128 * (bl + 1))
        bank, colh = pn_slot(h)
        p.emit("t", t.matmul(W[bank][:, colh:colh + 65],
                             W["SK"][:, tok], W["STBP"][:, 65 * h:65 * (h + 1)],
                             start=(h in (0, 7, 14)), stop=True, skip_group_check=True))
    for h in range(H):
        pf = h // 2
        tok = slice(256 * pf + 128 * bl, 256 * pf + 128 * (bl + 1))
        bank, colh = sd_slot(h)
        p.emit("t", t.matmul(W[bank][:, colh:colh + 65],
                             W["SQ"][:, tok], W["STBP"][:, 65 * h:65 * (h + 1)],
                             start=(h in (0, 7)), stop=True, skip_group_check=True))
    p.fence("t", waiters=("v",))

    vb4 = W["VB"].ap().rearrange("p (t2 h cc) -> p t2 h cc", t2=2, cc=65)
    ub3 = W["UB"].ap().rearrange("p (h cc) -> p h cc", cc=65)
    tb3f = W["TBF"].ap().rearrange("p (h cc) -> p h cc", cc=65)
    for s in range(4):
        rows = slice(32 * s, 32 * (s + 1))
        tc_num = 0
        if s > 0:
            t = p.pre("t")
            for h in range(H):
                bank, colh = pn_slot(h)
                p.emit("t", t.matmul(W[bank][rows, colh:colh + 65],
                                     W["GHB"][:, 256 * h + 128 + 32 * s:256 * h + 128 + 32 * (s + 1)],
                                     W["UB"][:, 65 * h:65 * (h + 1)],
                                     start=False, stop=True, tile_position=(0, 32 * s),
                                     skip_group_check=True))
            p.fence("t", waiters=("v",))
            t = p.pre("t")
            for h in range(H):
                bank, colh = sd_slot(h)
                p.emit("t", t.matmul(W[bank][rows, colh:colh + 65],
                                     W["GHB"][:, 256 * h + 32 * s:256 * h + 32 * (s + 1)],
                                     W["UB"][:, 65 * h:65 * (h + 1)],
                                     start=False, stop=True, tile_position=(0, 32 * s),
                                     skip_group_check=True))
            tc_num = p.fence("t", waiters=())
        v = p.pre("v")
        for bi, (bank, nh, base) in enumerate(PNBANKS):
            dens = W[bank][:, base:base + 65 * nh].rearrange("p (hh e) -> p hh e", e=65)[rows, 0:nh, 64:65]
            p.emit("v", v.reciprocal(out=bc_inner(W["RUF"][rows, 7 * bi:7 * bi + nh], 1),
                                     in_=dens))
        p.mini("v")
        for bi, (bank, nh, base) in enumerate(PNBANKS):
            nums = W[bank][:, base:base + 65 * nh].rearrange("p (hh e) -> p hh e", e=65)[rows, 0:nh, 0:64]
            p.emit("v", v.tensor_tensor(out=tb3f[rows, 7 * bi:7 * bi + nh, 0:64], in0=nums,
                                        in1=bc_inner(W["RUF"][rows, 7 * bi:7 * bi + nh], 64),
                                        op=ALU.mult))
        p.mini("v")
        if s > 0:
            p.wait_cnt("v", "t", tc_num)
        v = p.pre("v")
        p.emit("v", v.tensor_tensor(out=ub3[rows, :, 0:65], in0=vb4[rows, bl, :, 0:65],
                                    in1=tb3f[rows, :, 0:65], op=ALU.subtract))
        p.fence("v", waiters=("t",))

    p.wait_cnt("t", "a", ac_g3)
    p.wait_cnt("t", "a", L["ac_skt"])
    t = p.pre("t")
    for h in range(H):
        pr, half = h // 2, h % 2
        bank = W["pb1"] if pr < 4 else W["pb0"]
        colp = 128 * (pr % 4)
        p.emit("t", t.matmul(bank[64 * half:64 * (half + 1), colp:colp + 65],
                             W["SKT"][:, 1024 * bl + 128 * pr + 64 * half:1024 * bl + 128 * pr + 64 * (half + 1)],
                             W["UB"][:, 65 * h:65 * (h + 1)],
                             start=True, stop=True, tile_position=(0, 64 * half),
                             skip_group_check=True))
    p.fence("t", waiters=("v",))
    v = p.pre("v")
    for bi, bank in enumerate((W["pb1"], W["pb0"])):
        src = bank.ap().rearrange("p (pr e) -> p pr e", e=128)[:, 0:4, 0:65]
        dst = W["STF"].ap().rearrange("p (pr e) -> p pr e", e=65)[:, 4 * bi:4 * bi + 4, :]
        p.emit("v", v.tensor_tensor(out=dst, in0=src, in1=dst, op=ALU.add))
    p.mini("v")
    stf3 = W["STF"].ap().rearrange("p (pr c) -> p pr c", c=65)
    stbp2 = W["STBP"].ap().rearrange("p (m r) -> p m r", r=130)
    p.emit("v", v.tensor_copy(out=stbp2[0:64, :, 0:65], in_=stf3[0:64, :, :]))
    p.emit("v", v.tensor_copy(out=stbp2[64:128, :, 65:130], in_=stf3[64:128, :, :]))
    vc_stf = p.fence("v", waiters=())

    p.wait_cnt("t", "a", ac_exp1)
    t = p.pre("t")
    for h in range(H):
        bank, colh = pn_slot(h)
        p.emit("t", t.matmul(W[bank][:, colh:colh + 65],
                             W["EB"][:, 128 * h:128 * (h + 1)],
                             W["VB"][:, 1040 * bl + 65 * h:1040 * bl + 65 * (h + 1)],
                             start=(h in (0, 7, 14)), stop=True, skip_group_check=True))
    p.fence("t", waiters=("v",))

    v = p.pre("v")
    for bi, (bank, nh, base) in enumerate(SDBANKS):
        dens = W[bank][:, base:base + 65 * nh].rearrange("p (hh e) -> p hh e", e=65)[:, 0:nh, 64:65]
        p.emit("v", v.tensor_scalar(out=bc_inner(W["RQF"][:, 7 * bi:7 * bi + nh], 1),
                                    in0=dens, scalar1=1e-6, scalar2=None,
                                    op0=ALU.add, op1=ALU.bypass))
    p.mini("v")
    p.emit("v", v.reciprocal(out=W["RUF"][:, 0:16], in_=W["RQF"][:, 0:16]))
    p.mini("v")
    p.emit("v", v.tensor_tensor(out=W["RQF"][:, 0:16], in0=W["RUF"][:, 0:16],
                                in1=W["BETA"][:, :], op=ALU.mult))
    p.mini("v")
    for bi, (bank, nh, base) in enumerate(PNBANKS):
        dens = W[bank][:, base:base + 65 * nh].rearrange("p (hh e) -> p hh e", e=65)[:, 0:nh, 64:65]
        p.emit("v", v.reciprocal(out=bc_inner(W["RUF"][:, 7 * bi:7 * bi + nh], 1), in_=dens))
    p.mini("v")
    p.emit("v", v.tensor_tensor(out=W["RUF"][:, 0:16], in0=W["RUF"][:, 0:16],
                                in1=W["BETB"][:, :], op=ALU.mult))
    p.mini("v")
    tm3a = W["TMPA"].ap().rearrange("p (hh e) -> p hh e", e=64)
    tm3b = W["TMPB"].ap().rearrange("p (hh e) -> p hh e", e=64)
    for bi, (bank, nh, base) in enumerate(SDBANKS):
        nums = W[bank][:, base:base + 65 * nh].rearrange("p (hh e) -> p hh e", e=65)[:, 0:nh, 0:64]
        p.emit("v", v.tensor_tensor(out=tm3a[:, 7 * bi:7 * bi + nh, :], in0=nums,
                                    in1=bc_inner(W["RQF"][:, 7 * bi:7 * bi + nh], 64),
                                    op=ALU.mult))
    for bi, (bank, nh, base) in enumerate(PNBANKS):
        nums = W[bank][:, base:base + 65 * nh].rearrange("p (hh e) -> p hh e", e=65)[:, 0:nh, 0:64]
        p.emit("v", v.tensor_tensor(out=tm3b[:, 7 * bi:7 * bi + nh, :], in0=nums,
                                    in1=bc_inner(W["RUF"][:, 7 * bi:7 * bi + nh], 64),
                                    op=ALU.mult))
    p.mini("v")
    p.emit("v", v.tensor_tensor(out=W["OB"][:, 1024 * bl:1024 * (bl + 1)],
                                in0=W["TMPA"][:, :], in1=W["TMPB"][:, :], op=ALU.add))
    vc_comb = p.fence("v", waiters=())

    return {"vc_comb": vc_comb, "vc_stf": vc_stf, "ac_exp1": ac_exp1}


def epilogue(nc, p, sb, L):
    s = p.pre("s")
    s.wait_ge(p.dsem, 16 * p.dcnt)
    if p.dcnt_g:
        s.wait_ge(p.dsem_g, 16 * p.dcnt_g)


_NC_CACHE = {}


def kernel(L0_attn_x, L0_attn_w_qkv, L0_attn_w_out, L0_attn_ln_g, L0_attn_ln_b,
           L0_attn_mem_beta, L0_ffn_rhs):
    from concourse.bass_utils import run_bass_kernel_spmd
    x = np.asarray(L0_attn_x, dtype=np.float32)
    base = make_inputs(x[0], L0_attn_w_qkv, L0_attn_w_out, L0_attn_ln_g,
                       L0_attn_ln_b, L0_attn_mem_beta, L0_ffn_rhs)
    in_maps = []
    for bidx in range(8):
        m = dict(base)
        m["x"] = np.ascontiguousarray(x[bidx])
        in_maps.append(m)
    if "nc" not in _NC_CACHE:
        _NC_CACHE["nc"] = build_core(debug_dump=[], n_chunks=NCH)
    res = run_bass_kernel_spmd(_NC_CACHE["nc"], in_maps, core_ids=list(range(8)))
    out = np.stack([np.asarray(r["out"], dtype=np.float32) for r in res.results], axis=0)
    return out
